# version 04
# baseline (speedup 1.0000x reference)
import numpy as np
import ml_dtypes
from contextlib import ExitStack

import concourse.bass as bass
import concourse.mybir as mybir
from concourse.bass_utils import run_bass_kernel_spmd

F32 = mybir.dt.float32
BF16 = mybir.dt.bfloat16
AF = mybir.ActivationFunctionType
ALU = mybir.AluOpType
AX = mybir.AxisListType

D = 1024
H = 16
HD = 64
KPH = 32
E = 8
FF = 2048
SHIFT_ATTN = np.float32(-16.0)
SHIFT_FFN = np.float32(-16.0)
SCALE_ATTN = np.float32(1.0 / 8.0)
SCALE_FFN = np.float32(1.0 / np.sqrt(32.0))


def host_consts(S):
    T = S // 128
    half = HD // 2
    inv_freq = 1.0 / (10000.0 ** (np.arange(half, dtype=np.float64) / half))
    t = np.arange(S, dtype=np.float64)
    freqs = t[:, None] * inv_freq[None, :]
    cos = np.cos(freqs).astype(np.float32)
    sin = np.sin(freqs).astype(np.float32)
    cosT = np.zeros((128, 32 * T), np.float32)
    sinT = np.zeros((128, 32 * T), np.float32)
    for tt in range(T):
        cosT[:, tt * 32:(tt + 1) * 32] = cos[tt * 128:(tt + 1) * 128]
        sinT[:, tt * 32:(tt + 1) * 32] = sin[tt * 128:(tt + 1) * 128]
    masks = np.zeros((128, 4 * 512), ml_dtypes.bfloat16)
    tp = np.arange(128)[:, None]
    sp = np.arange(512)[None, :]
    for j in range(4):
        masks[:, j * 512:(j + 1) * 512] = ((j * 128 + tp) <= sp).astype(
            ml_dtypes.bfloat16)
    idn = np.eye(128).astype(ml_dtypes.bfloat16)
    one1f = np.ones((1, 128), np.float32)
    return {"cosT": cosT, "sinT": sinT, "masks": masks, "idn": idn,
            "one1f": one1f}


class Prog:
    ENGINES = ("tensor", "scalar", "vector", "gpsimd", "sync")

    def __init__(self, nc, ctx):
        self.nc = nc
        self.plan = {e: [] for e in self.ENGINES}
        self.sems = {}
        self.cnt = {}
        for name in ("T", "S", "V", "G", "Y"):
            self.sems[name] = ctx.enter_context(nc.semaphore("sem" + name))
            self.cnt[name] = 0

    def wait(self, eng, sem, tok):
        if tok is None or tok <= 0:
            return
        h = self.sems[sem]
        self.plan[eng].append(lambda Eg: Eg.wait_ge(h, tok))

    SERIALIZE = False
    SER_TAGS = set()
    ENG_SEM = {"tensor": "T", "scalar": "S", "vector": "V",
               "gpsimd": "G", "sync": "Y"}
    cur_tag = "init"

    def tag(self, t):
        self.cur_tag = t

    def op(self, eng, f, sem=None, inc=1):
        if self.SERIALIZE or (self.cur_tag in self.SER_TAGS):
            for sn in ("T", "S", "V", "G", "Y"):
                self.wait(eng, sn, self.cnt[sn])
        if sem is None:
            self.plan[eng].append(f)
            return None
        h = self.sems[sem]
        self.cnt[sem] += inc
        tok = self.cnt[sem]
        self.plan[eng].append(lambda Eg, f=f: f(Eg).then_inc(h, inc))
        return tok

    def dma(self, eng, out_ap, in_ap, sem, **kw):
        return self.op(eng, lambda Eg: Eg.dma_start(out_ap, in_ap, **kw),
                       sem=sem, inc=16)

    def mm_group(self, mms, sem="T"):
        for f in mms[:-1]:
            self.op("tensor", f)
        return self.op("tensor", mms[-1], sem=sem)

    def snap(self):
        return dict(self.cnt)

    def wait_all(self, eng, toks):
        for sname, tok in toks.items():
            self.wait(eng, sname, tok)

    def emit(self, block):
        for eng in self.ENGINES:
            seq = self.plan[eng]

            def body(Eg, seq=seq):
                for f in seq:
                    f(Eg)
            getattr(block, eng)(body)


class _Stop(Exception):
    pass


def build_program(S, out_stage="full"):
    nc = bass.Bass("TRN2", target_bir_lowering=False)
    T = S // 128
    NB = S // 512
    ST = 512

    x_h = nc.dram_tensor("x", [S, D], F32, kind="ExternalInput")
    wqkv_h = nc.dram_tensor("w_qkv", [3 * D, D], F32, kind="ExternalInput")
    wout_h = nc.dram_tensor("w_out", [D, D], F32, kind="ExternalInput")
    qw_h = nc.dram_tensor("fq_w", [D, D], F32, kind="ExternalInput")
    qb_h = nc.dram_tensor("fq_b", [1, D], F32, kind="ExternalInput")
    kw_h = nc.dram_tensor("fk_w", [D, 512], F32, kind="ExternalInput")
    kb_h = nc.dram_tensor("fk_b", [1, 512], F32, kind="ExternalInput")
    vw_h = nc.dram_tensor("fv_w", [D, 512], F32, kind="ExternalInput")
    vb_h = nc.dram_tensor("fv_b", [1, 512], F32, kind="ExternalInput")
    ow_h = nc.dram_tensor("fo_w", [D, D], F32, kind="ExternalInput")
    ob_h = nc.dram_tensor("fo_b", [1, D], F32, kind="ExternalInput")
    rw_h = nc.dram_tensor("fr_w", [D, E], F32, kind="ExternalInput")
    rb_h = nc.dram_tensor("fr_b", [1, E], F32, kind="ExternalInput")
    w1_h = nc.dram_tensor("fw1", [E, D, FF], F32, kind="ExternalInput")
    b1_h = nc.dram_tensor("fb1", [E, FF], F32, kind="ExternalInput")
    w2_h = nc.dram_tensor("fw2", [E, FF, D], F32, kind="ExternalInput")
    b2_h = nc.dram_tensor("fb2", [E, D], F32, kind="ExternalInput")
    cosT_h = nc.dram_tensor("cosT", [128, 32 * T], F32, kind="ExternalInput")
    sinT_h = nc.dram_tensor("sinT", [128, 32 * T], F32, kind="ExternalInput")
    masks_h = nc.dram_tensor("masks", [128, 4 * 512], BF16, kind="ExternalInput")
    idn_h = nc.dram_tensor("idn", [128, 128], BF16, kind="ExternalInput")
    one1f_h = nc.dram_tensor("one1f", [1, 128], F32, kind="ExternalInput")
    qw32_h = nc.dram_tensor("fq_w32", [D, 512], F32, kind="ExternalInput")
    qb32T_h = nc.dram_tensor("fq_b32T", [128, 6], F32, kind="ExternalInput")
    kbT_h = nc.dram_tensor("fk_bT", [128, 6], F32, kind="ExternalInput")
    obT_h = nc.dram_tensor("fo_bT", [128, 8], F32, kind="ExternalInput")
    b1T_h = nc.dram_tensor("fb1T", [128, 128], F32, kind="ExternalInput")
    rbB_h = nc.dram_tensor("fr_bB", [128, 8], F32, kind="ExternalInput")
    oneb_h = nc.dram_tensor("oneb", [1, 128], BF16, kind="ExternalInput")

    out_h = nc.dram_tensor("out", [S, D], F32, kind="ExternalOutput")

    ao_scr = nc.dram_tensor("ao_scr", [D, S], BF16, kind="Internal")
    aof_scr = nc.dram_tensor("aof_scr", [512, S], BF16, kind="Internal")
    y1_scr = nc.dram_tensor("y1_scr", [S, D], F32, kind="Internal")
    moe_scr = nc.dram_tensor("moe_scr", [S, D], F32, kind="Internal")

    with ExitStack() as ctx:
        en = ctx.enter_context
        P = Prog(nc, ctx)

        idn = en(nc.sbuf_tensor("idn_s", [128, 128], BF16))
        one1f = en(nc.sbuf_tensor("one1f_s", [1, 128], F32))
        onebs = en(nc.sbuf_tensor("onebs_s", [1, 128], BF16))
        for dst, src in ((idn, idn_h), (one1f, one1f_h), (onebs, oneb_h)):
            P.dma("sync", dst[:, :], src[:, :], "Y")
        tok_consts = P.cnt["Y"]


        phase_stacks = []

        def new_phase():
            st_ = ExitStack()
            phase_stacks.append(st_)
            return st_

        def close_phase(st_):
            st_.close()
            phase_stacks.remove(st_)

        def transpose_load(src_h, n_rt, dst_sb, dst_cols, bufs, psb, state,
                           dma_eng="gpsimd"):
            for r in range(n_rt):
                buf = bufs[r % 2]
                if state["ld"][r % 2]:
                    P.wait(dma_eng, "T", state["ld"][r % 2])
                tok_l = P.dma(dma_eng, buf[:, :],
                              src_h[r * 128:(r + 1) * 128, :], "G")
                last_t = None
                for dc in range(8):
                    pb = psb[dc % 2]
                    if dc == 0:
                        P.wait("tensor", "G", tok_l)
                    P.wait("tensor", "V", state["ev"][dc % 2])
                    last_t = P.op(
                        "tensor",
                        lambda Eg, buf=buf, dc=dc, pb=pb:
                        Eg.transpose(pb[:, :], buf[:, dc * 128:(dc + 1) * 128],
                                     idn[:, :]), sem="T")
                    P.wait("vector", "T", last_t)
                    state["ev"][dc % 2] = P.op(
                        "vector",
                        lambda Eg, pb=pb, dc=dc, r=r:
                        Eg.tensor_copy(
                            dst_sb[:, dc * dst_cols + r * 128:
                                   dc * dst_cols + (r + 1) * 128], pb[:, :]),
                        sem="V")
                state["ld"][r % 2] = last_t
            return max(state["ev"])

        def dump_and_stop(items):
            pdd = new_phase()
            dbgd = pdd.enter_context(nc.sbuf_tensor("dbgdump", [128, S], F32))
            snapd = P.snap()
            P.wait_all("vector", {k: snapd[k] for k in ("T", "S", "G", "Y")})
            prevd = 0
            for (srcd, row0, ncols) in items:
                npart = srcd.shape[0]
                P.wait("vector", "G", prevd)
                tokc = P.op("vector", lambda Eg, srcd=srcd, ncols=ncols,
                            npart=npart:
                            Eg.tensor_copy(dbgd[0:npart, 0:ncols], srcd),
                            sem="V")
                P.wait("gpsimd", "V", tokc)
                prevd = P.dma("gpsimd", out_h[row0:row0 + npart, 0:ncols],
                              dbgd[0:npart, 0:ncols], "G")
            raise _Stop


        try:
            pA = new_phase()
            cosT = pA.enter_context(nc.sbuf_tensor("cosT_s", [128, 32 * T], F32))
            sinT = pA.enter_context(nc.sbuf_tensor("sinT_s", [128, 32 * T], F32))
            masks = pA.enter_context(nc.sbuf_tensor("masks_s", [128, 4 * 512], BF16))
            for dst_c, src_c in ((cosT, cosT_h), (sinT, sinT_h),
                                 (masks, masks_h)):
                P.dma("sync", dst_c[:, :], src_c[:, :], "Y")
            tok_consts2 = P.cnt["Y"]
            P.wait("vector", "Y", tok_consts2)
            xfm = pA.enter_context(nc.sbuf_tensor("xfm", [128, 8 * S], BF16))
            wqkvT = pA.enter_context(nc.sbuf_tensor("wqkvT", [128, 8 * 3 * D], BF16))

            p0 = new_phase()
            ld0 = p0.enter_context(nc.sbuf_tensor("ld0", [128, D], BF16))
            ld1 = p0.enter_context(nc.sbuf_tensor("ld1", [128, D], BF16))
            pt0 = p0.enter_context(nc.psum_tensor("pt0", [128, 128], BF16))
            pt1 = p0.enter_context(nc.psum_tensor("pt1", [128, 128], BF16))
            st0 = {"ld": [0, 0], "ev": [0, 0]}
            P.wait("tensor", "Y", tok_consts)
            tok_xfm = transpose_load(x_h, T, xfm, S, (ld0, ld1), (pt0, pt1), st0)
            tok_wqkvT = transpose_load(wqkv_h, 24, wqkvT, 3 * D, (ld0, ld1),
                                       (pt0, pt1), st0)
            close_phase(p0)
            if out_stage == "xfm":
                dump_and_stop([(xfm[:, dc * S:dc * S + S], dc * 128, S)
                               for dc in range(8)])
            if out_stage == "wq":
                dump_and_stop([(wqkvT[:, dc * 3 * D:dc * 3 * D + D], dc * 128, D)
                               for dc in range(8)])

            p1 = new_phase()
            Kg = p1.enter_context(nc.sbuf_tensor("Kg", [128, 2 * S], BF16))
            Vg = p1.enter_context(nc.sbuf_tensor("Vg", [128, T * 260], BF16))
            Qg = p1.enter_context(nc.sbuf_tensor("Qg", [128, 2 * ST], BF16))
            Pbuf = p1.enter_context(nc.sbuf_tensor("Pbuf", [128, 2 * 2048], BF16))
            tmp1 = p1.enter_context(nc.sbuf_tensor("tmp1", [128, 256], F32))
            tmp2 = p1.enter_context(nc.sbuf_tensor("tmp2", [128, 256], F32))
            ktm = p1.enter_context(nc.sbuf_tensor("ktm", [128, 256], BF16))
            avs_b = [p1.enter_context(nc.sbuf_tensor(f"avs{i}", [64, 512], F32))
                     for i in range(2)]
            rcp_b = [p1.enter_context(nc.sbuf_tensor(f"rcp{i}", [1, 512], F32))
                     for i in range(2)]
            rcpbf = [p1.enter_context(nc.sbuf_tensor(f"rcpbf{i}", [1, 512], BF16))
                     for i in range(2)]
            AOp = [p1.enter_context(nc.sbuf_tensor(f"AOp{i}", [128, S], BF16))
                   for i in range(2)]
            ps_kv = p1.enter_context(nc.psum_tensor("ps_kv", [128, 512], F32))
            ps_t = [p1.enter_context(nc.psum_tensor("ps_t0", [128, 128], BF16))]
            ps_t = ps_t * 2
            ps_sc = [p1.enter_context(nc.psum_tensor("ps_sc0", [128, 2048], F32))]
            ps_avv = [p1.enter_context(nc.psum_tensor(f"ps_av{i}", [65, 512], F32))
                      for i in range(2)]

            def rope_tm(n_heads, hdim, ttile, ncols):
                halfd = hdim // 2
                psv = ps_kv[:, 0:ncols].rearrange("p (h d) -> p h d", h=n_heads)
                x1, x2 = psv[:, :, 0:halfd], psv[:, :, halfd:hdim]
                ov = ktm[:, 0:ncols].rearrange("p (h d) -> p h d", h=n_heads)
                o1, o2 = ov[:, :, 0:halfd], ov[:, :, halfd:hdim]
                c = cosT[:, ttile * 32:(ttile + 1) * 32].rearrange(
                    "p (o i) -> p o i", o=1)
                s = sinT[:, ttile * 32:(ttile + 1) * 32].rearrange(
                    "p (o i) -> p o i", o=1)
                t1v = tmp1[:, 0:ncols].rearrange("p (h d) -> p h d", h=n_heads)
                t1a, t1b = t1v[:, :, 0:halfd], t1v[:, :, halfd:hdim]
                t2v = tmp2[:, 0:ncols].rearrange("p (h d) -> p h d", h=n_heads)
                t2a, t2b = t2v[:, :, 0:halfd], t2v[:, :, halfd:hdim]

                def tt(Eg, o, a, b, op):
                    aa, bb = bass.broadcast_tensor_aps(a, b)
                    return Eg.tensor_tensor(o, aa, bb, op)
                P.op("vector", lambda Eg: tt(Eg, t1a, x1, c, ALU.mult))
                P.op("vector", lambda Eg: tt(Eg, t1b, x2, s, ALU.mult))
                P.op("vector", lambda Eg: tt(Eg, t2a, x1, s, ALU.mult))
                tm = P.op("vector", lambda Eg: tt(Eg, t2b, x2, c, ALU.mult),
                          sem="V")
                P.wait("vector", "V", tm)
                ts_ = P.op("vector", lambda Eg: Eg.tensor_sub(o1, t1a, t1b),
                           sem="V")
                P.wait("vector", "V", ts_)
                return P.op("vector", lambda Eg: Eg.tensor_add(o2, t2a, t2b),
                            sem="V")

            kv_free = 0
            kv_free_T = 0
            tp_war = [0, 0]
            sc_war = [0]
            pbuf_war = [0, 0]
            av_war_s = [0, 0]
            av_war_v = [0, 0]
            av_inst = [0]
            bc_war = 0
            ao_fill = [0, 0]
            aop_war = [0, 0]
            tok_Kg = 0
            tok_Vg = 0
            tok_Qg = 0
            prev_group_av = 0

            def produce_tm(tt_i, wcol0, out_to, n_out):
                mms = []
                for dc in range(8):
                    mms.append(lambda Eg, dc=dc, tt_i=tt_i, wcol0=wcol0, n_out=n_out:
                               Eg.matmul(ps_kv[:, 0:n_out],
                                         xfm[:, dc * S + tt_i * 128:
                                             dc * S + (tt_i + 1) * 128],
                                         wqkvT[:, dc * 3 * D + wcol0:
                                               dc * 3 * D + wcol0 + n_out],
                                         start=(dc == 0), stop=(dc == 7)))
                return P.mm_group(mms)

            for g in range(4):
                P.tag("kprod")
                for tt_i in range(T):
                    P.wait("tensor", "V", kv_free)
                    P.wait("tensor", "V", bc_war)
                    if g == 0 and tt_i == 0:
                        P.wait("tensor", "V", tok_xfm)
                        P.wait("tensor", "V", tok_wqkvT)
                    tokK = produce_tm(tt_i, D + g * 256, None, 256)
                    P.wait("vector", "T", tokK)
                    kv_free = rope_tm(4, HD, tt_i, 256)
                    for ch in range(2):
                        pb = ps_t[ch]
                        P.wait("tensor", "V", kv_free)
                        P.wait("tensor", "V", max(tp_war))
                        tok_tp = P.op("tensor", lambda Eg, ch=ch, pb=pb:
                                      Eg.transpose(pb[:, :],
                                                   ktm[:, ch * 128:(ch + 1) * 128],
                                                   idn[:, :]), sem="T")
                        P.wait("vector", "T", tok_tp)
                        tp_war[ch] = P.op(
                            "vector", lambda Eg, ch=ch, tt_i=tt_i, pb=pb:
                            Eg.tensor_copy(Kg[:, ch * S + tt_i * 128:
                                              ch * S + (tt_i + 1) * 128], pb[:, :]),
                            sem="V")
                tok_Kg = tp_war[1]
                if out_stage == "kg0" and g == 0:
                    dump_and_stop([(Kg[:, ch * S:ch * S + S], ch * 128, S)
                                   for ch in range(2)])
                P.tag("vprod")
                P.wait("vector", "T", prev_group_av)
                tok_vset = P.op("vector", lambda Eg: Eg.memset(Vg[:, :], 1.0),
                                sem="V")
                for tt_i in range(T):
                    P.wait("tensor", "V", kv_free)
                    P.wait("tensor", "V", bc_war)
                    tokV = produce_tm(tt_i, 2 * D + g * 256, None, 256)
                    P.wait("vector", "T", tokV)
                    if tt_i == 0:
                        P.wait("vector", "V", tok_vset)
                    for m6 in range(4):
                        kv_free = P.op(
                            "vector", lambda Eg, tt_i=tt_i, m6=m6:
                            Eg.tensor_copy(
                                Vg[:, tt_i * 260 + m6 * 65:
                                   tt_i * 260 + m6 * 65 + 64],
                                ps_kv[:, m6 * 64:(m6 + 1) * 64]),
                            sem="V" if m6 == 3 else None)
                tok_Vg = kv_free
                if out_stage == "vg0" and g == 0:
                    dump_and_stop([(Vg[:, tt * 260:tt * 260 + 260], tt * 128, 260)
                                   for tt in range(min(T, 8))])

                for sb in range(NB):
                    P.tag("qprod")
                    for st_i in range(4):
                        ts_abs = sb * 4 + st_i
                        P.wait("tensor", "V", kv_free)
                        P.wait("tensor", "V", bc_war)
                        tokQ = produce_tm(ts_abs, g * 256, None, 256)
                        P.wait("vector", "T", tokQ)
                        kv_free = rope_tm(4, HD, ts_abs, 256)
                        for ch in range(2):
                            pb = ps_t[ch]
                            P.wait("tensor", "V", kv_free)
                            P.wait("tensor", "V", max(tp_war))
                            tok_tp = P.op("tensor", lambda Eg, ch=ch, pb=pb:
                                          Eg.transpose(
                                              pb[:, :],
                                              ktm[:, ch * 128:(ch + 1) * 128],
                                              idn[:, :]), sem="T")
                            P.wait("vector", "T", tok_tp)
                            tp_war[ch] = P.op(
                                "vector", lambda Eg, ch=ch, st_i=st_i, pb=pb:
                                Eg.tensor_copy(Qg[:, ch * ST + st_i * 128:
                                                  ch * ST + (st_i + 1) * 128],
                                               pb[:, :]), sem="V")
                    tok_Qg = tp_war[1]
                    if out_stage == "qg0" and g == 0 and sb == 0:
                        dump_and_stop([(Qg[:, ch * ST:ch * ST + 512], ch * 128, 512)
                                       for ch in range(2)])

                    P.tag("attn")
                    ntc = (sb + 1) * 4
                    for pp in range(2):
                        kch = pp
                        batches = [(b, [b, b + 1]) for b in range(0, ntc, 2)]

                        def emit_scores(bi):
                            tcb, tcs = batches[bi]
                            P.wait("tensor", "V", tok_Qg)
                            P.wait("tensor", "V", tok_Kg)
                            P.wait("tensor", "S", sc_war[0])
                            mms = []
                            for j, tc in enumerate(tcs):
                                for m, ksub in enumerate((0, 64)):
                                    mms.append(
                                        lambda Eg, tc=tc, j=j, m=m, ksub=ksub,
                                        kch=kch:
                                        Eg.matmul(
                                            ps_sc[0][:, (m * 2 + j) * 512:
                                                     (m * 2 + j + 1) * 512],
                                            Kg[ksub:ksub + 64,
                                               kch * S + tc * 128:
                                               kch * S + (tc + 1) * 128],
                                            Qg[ksub:ksub + 64,
                                               kch * ST: kch * ST + 512],
                                            start=True, stop=True,
                                            skip_group_check=True))
                            tok_sc = P.mm_group(mms)
                            buf_i = bi % 2
                            P.wait("scalar", "T", tok_sc)
                            P.wait("scalar", "T", pbuf_war[buf_i])
                            tok_e = P.op(
                                "scalar", lambda Eg, buf_i=buf_i:
                                Eg.activation(
                                    Pbuf[:, buf_i * 2048: buf_i * 2048 + 2048],
                                    ps_sc[0][:, :], AF.Exp,
                                    bias=SHIFT_ATTN, scale=SCALE_ATTN), sem="S")
                            sc_war[0] = tok_e
                            mask_tok = None
                            for j, tc in enumerate(tcs):
                                if tc < sb * 4:
                                    continue
                                jj = tc - sb * 4
                                for m in range(2):
                                    P.wait("vector", "S", tok_e)
                                    mask_tok = P.op(
                                        "vector",
                                        lambda Eg, buf_i=buf_i, j=j, jj=jj, m=m:
                                        Eg.tensor_mul(
                                            Pbuf[:, buf_i * 2048 + (m * 2 + j) * 512:
                                                 buf_i * 2048 + (m * 2 + j + 1) * 512],
                                            Pbuf[:, buf_i * 2048 + (m * 2 + j) * 512:
                                                 buf_i * 2048 + (m * 2 + j + 1) * 512],
                                            masks[:, jj * 512:(jj + 1) * 512]),
                                        sem="V")
                            return (tok_e, mask_tok, buf_i)

                        def emit_av(bi, meta, first):
                            tcb, tcs = batches[bi]
                            tok_e, mask_tok, buf_i = meta
                            if mask_tok is not None:
                                P.wait("tensor", "V", mask_tok)
                            else:
                                P.wait("tensor", "S", tok_e)
                            if first:
                                P.wait("tensor", "V", tok_Vg)
                                for m in range(2):
                                    P.wait("tensor", "S", av_war_s[m])
                                    P.wait("tensor", "V", av_war_v[m])
                            mms = []
                            for j, tc in enumerate(tcs):
                                for m in range(2):
                                    hh = kch * 2 + m
                                    mms.append(
                                        lambda Eg, tc=tc, j=j, m=m, hh=hh,
                                        ntc=ntc, buf_i=buf_i,
                                        first=(first and j == 0):
                                        Eg.matmul(
                                            ps_avv[m][:, :],
                                            Vg[:, tc * 260 + hh * 65:
                                               tc * 260 + (hh + 1) * 65],
                                            Pbuf[:, buf_i * 2048 + (m * 2 + j) * 512:
                                                 buf_i * 2048 + (m * 2 + j + 1) * 512],
                                            start=first, stop=(tc == ntc - 1),
                                            skip_group_check=True))
                            tok = P.mm_group(mms)
                            pbuf_war[buf_i] = tok
                            return tok

                        metas = {}
                        tok_av = None
                        for bi in range(len(batches)):
                            metas[bi] = emit_scores(bi)
                            if bi >= 1:
                                tok_av = emit_av(bi - 1, metas[bi - 1], bi == 1)
                        tok_av = emit_av(len(batches) - 1, metas[len(batches) - 1],
                                         len(batches) == 1)
                        P.tag("fin")
                        for m in range(2):
                            hh = kch * 2 + m
                            pav = ps_avv[m]
                            avs = avs_b[m]
                            rcp = rcp_b[m]
                            rbf = rcpbf[m]
                            P.wait("vector", "T", tok_av)
                            t_rc = P.op("vector", lambda Eg, rcp=rcp, pav=pav:
                                        Eg.reciprocal(rcp[:, :], pav[64:65, :]),
                                        sem="V")
                            P.wait("vector", "V", t_rc)
                            av_war_v[m] = P.op(
                                "vector", lambda Eg, rbf=rbf, rcp=rcp:
                                Eg.tensor_copy(rbf[:, :], rcp[:, :]), sem="V")
                            P.wait("scalar", "T", tok_av)
                            P.wait("scalar", "V", bc_war)
                            av_war_s[m] = P.op(
                                "scalar", lambda Eg, avs=avs, pav=pav:
                                Eg.copy(avs[:, :], pav[0:64, :]), sem="S")
                            P.wait("tensor", "V", av_war_v[m])
                            P.wait("tensor", "V", bc_war)
                            P.wait("tensor", "V", kv_free)
                            tok_bc = P.op("tensor", lambda Eg, rbf=rbf:
                                          Eg.matmul(ps_kv[0:64, :],
                                                    onebs[:, 0:64],
                                                    rbf[:, :], start=True,
                                                    stop=True,
                                                    skip_group_check=True),
                                          sem="T")
                            kv_free_T = tok_bc
                            pair_i = hh // 2
                            dst = AOp[pair_i]
                            prow = (hh % 2) * 64
                            P.wait("vector", "T", tok_bc)
                            P.wait("vector", "S", av_war_s[m])
                            P.wait("vector", "G", aop_war[pair_i])
                            bc_war = P.op(
                                "vector", lambda Eg, dst=dst, prow=prow, sb=sb,
                                avs=avs:
                                Eg.tensor_mul(
                                    dst[prow:prow + 64, sb * ST:(sb + 1) * ST],
                                    avs[:, :], ps_kv[0:64, :]), sem="V")
                            ao_fill[pair_i] = bc_war
                            prev_group_av = tok_av
                P.tag("spill")
                for pi in range(2):
                    a_chunk = g * 2 + pi
                    P.wait("gpsimd", "V", ao_fill[pi])
                    aop_war[pi] = P.dma(
                        "gpsimd", ao_scr[a_chunk * 128:(a_chunk + 1) * 128, :],
                        AOp[pi][:, :], "G")
            tok_ao_spill = P.cnt["G"]
            end_p1 = P.snap()
            close_phase(p1)
            close_phase(pA)

            P.tag("p2")
            pB = new_phase()
            y1fm = pB.enter_context(nc.sbuf_tensor("y1fm", [128, 8 * S], BF16))
            p2 = new_phase()
            woutT = p2.enter_context(nc.sbuf_tensor("woutT", [128, 8 * D], BF16))
            ao_all = p2.enter_context(nc.sbuf_tensor("ao_all", [128, 8 * S], BF16))
            ld2 = [p2.enter_context(nc.sbuf_tensor(f"ld2_{i}", [128, D], BF16))
                   for i in range(2)]
            pt2 = [p2.enter_context(nc.psum_tensor(f"pt2_{i}", [128, 128], BF16))
                   for i in range(2)]
            P.wait_all("gpsimd", end_p1)
            P.wait_all("sync", end_p1)
            st2 = {"ld": [0, 0], "ev": [0, 0]}
            tok_woutT = transpose_load(wout_h, 8, woutT, D, (ld2[0], ld2[1]),
                                       (pt2[0], pt2[1]), st2)
            ao_toks = []
            for ac in range(8):
                if ac == 0:
                    P.wait("gpsimd", "G", tok_ao_spill)
                ao_toks.append(P.dma("gpsimd", ao_all[:, ac * S:(ac + 1) * S],
                                     ao_scr[ac * 128:(ac + 1) * 128, :], "G"))
            tok_ao_all = ao_toks[-1]
            if out_stage == "ao":
                dump_and_stop([(ao_all[:, ac * S:ac * S + S], ac * 128, S)
                               for ac in range(8)])

            xt = [p2.enter_context(nc.sbuf_tensor(f"xt{i}", [128, D], F32))
                  for i in range(2)]
            zt = p2.enter_context(nc.sbuf_tensor("zt", [128, D], F32))
            zsq = p2.enter_context(nc.sbuf_tensor("zsq", [128, D], F32))
            stt = p2.enter_context(nc.sbuf_tensor("stt", [128, 8], F32))
            y1t = p2.enter_context(nc.sbuf_tensor("y1t", [128, D], F32))
            y1b = p2.enter_context(nc.sbuf_tensor("y1b", [128, D], BF16))
            ps_o = p2.enter_context(nc.psum_tensor("ps_o", [128, 1024], F32))

            xt_war = [0, 0]
            ln_war = 0
            y1dma_war = 0
            y1b_war = 0
            y1t_war = 0
            y1fm_evac = [0, 0]
            for ts_i in range(T):
                P.tag("p2b")
                bi = ts_i % 2
                P.wait("sync", "V", xt_war[bi])
                tok_x = P.dma("sync", xt[bi][:, :],
                              x_h[ts_i * 128:(ts_i + 1) * 128, :], "Y")
                mms = []
                for half in range(2):
                    for ac in range(8):
                        mms.append(
                            lambda Eg, ac=ac, half=half, ts_i=ts_i:
                            Eg.matmul(ps_o[:, half * 512:(half + 1) * 512],
                                      ao_all[:, ac * S + ts_i * 128:
                                             ac * S + (ts_i + 1) * 128],
                                      woutT[:, ac * D + half * 512:
                                            ac * D + (half + 1) * 512],
                                      start=(ac == 0), stop=(ac == 7),
                                      skip_group_check=True))
                if ts_i == 0:
                    P.wait("tensor", "V", tok_woutT)
                    P.wait("tensor", "G", tok_ao_all)
                P.wait("tensor", "V", ln_war)
                tok_o = P.mm_group(mms)
                P.wait("vector", "T", tok_o)
                P.wait("vector", "Y", tok_x)
                P.wait("vector", "V", y1t_war)
                P.op("vector", lambda Eg, bi=bi:
                     Eg.tensor_add(zt[:, 0:512], ps_o[:, 0:512], xt[bi][:, 0:512]))
                ln_war = P.op("vector", lambda Eg, bi=bi:
                              Eg.tensor_add(zt[:, 512:1024], ps_o[:, 512:1024],
                                            xt[bi][:, 512:1024]), sem="V")
                xt_war[bi] = ln_war
                P.tag("p2c")
                P.wait("vector", "V", ln_war)
                t1 = P.op("vector", lambda Eg:
                          Eg.reduce_sum(stt[:, 0:1], zt[:, :], axis=AX.X),
                          sem="V")
                P.wait("scalar", "V", ln_war)
                tok_sq = P.op("scalar", lambda Eg: Eg.square(zsq[:, :], zt[:, :]),
                              sem="S")
                P.wait("vector", "S", tok_sq)
                t2 = P.op("vector", lambda Eg:
                          Eg.reduce_sum(stt[:, 1:2], zsq[:, :], axis=AX.X),
                          sem="V")
                P.wait("vector", "V", t1)
                t3 = P.op("vector", lambda Eg: Eg.tensor_scalar(
                    stt[:, 2:3], stt[:, 0:1], 1.0 / D, None, ALU.mult), sem="V")
                P.wait("vector", "V", t2)
                t4 = P.op("vector", lambda Eg: Eg.tensor_scalar(
                    stt[:, 3:4], stt[:, 1:2], 1.0 / D, None, ALU.mult), sem="V")
                P.wait("vector", "V", t3)
                t5 = P.op("vector", lambda Eg:
                          Eg.tensor_mul(stt[:, 4:5], stt[:, 2:3], stt[:, 2:3]),
                          sem="V")
                P.wait("vector", "V", t4)
                P.wait("vector", "V", t5)
                tok_v = P.op("vector", lambda Eg:
                             Eg.tensor_sub(stt[:, 5:6], stt[:, 3:4], stt[:, 4:5]),
                             sem="V")
                P.wait("scalar", "V", tok_v)
                tok_ln = P.op("scalar", lambda Eg: Eg.activation(
                    stt[:, 6:7], stt[:, 5:6], AF.Ln, bias=np.float32(1e-5)),
                    sem="S")
                P.wait("scalar", "S", tok_ln)
                tok_r = P.op("scalar", lambda Eg: Eg.activation(
                    stt[:, 7:8], stt[:, 6:7], AF.Exp, scale=np.float32(-0.5)),
                    sem="S")
                P.wait("vector", "S", tok_r)
                t6 = P.op("vector", lambda Eg:
                          Eg.tensor_mul(stt[:, 0:1], stt[:, 2:3], stt[:, 7:8]),
                          sem="V")
                P.wait("vector", "Y", y1dma_war)
                P.wait("vector", "V", t6)
                P.op("vector", lambda Eg: Eg.tensor_scalar(
                    y1t[:, :], zt[:, :], stt[:, 7:8], stt[:, 0:1],
                    ALU.mult, ALU.subtract))
                P.tag("p2d")
                y1b_war = P.op("vector", lambda Eg:
                               Eg.tensor_copy(y1b[:, :], y1t[:, :]), sem="V")
                y1t_war = P.op("vector", lambda Eg: Eg.tensor_scalar(
                    y1t[:, :], y1t[:, :], 2.0, None, ALU.mult),
                    sem="V")
                P.wait("sync", "V", y1t_war)
                y1dma_war = P.dma("sync",
                                  y1_scr[ts_i * 128:(ts_i + 1) * 128, :],
                                  y1t[:, :], "Y")
                for dc in range(8):
                    pbi = dc % 2
                    pb = pt2[pbi]
                    P.wait("tensor", "V", y1b_war)
                    P.wait("tensor", "V", y1fm_evac[pbi])
                    tok_tp = P.op("tensor", lambda Eg, dc=dc, pb=pb:
                                  Eg.transpose(pb[:, :],
                                               y1b[:, dc * 128:(dc + 1) * 128],
                                               idn[:, :]), sem="T")
                    P.wait("vector", "T", tok_tp)
                    y1fm_evac[pbi] = P.op(
                        "vector", lambda Eg, dc=dc, ts_i=ts_i, pb=pb:
                        Eg.tensor_copy(y1fm[:, dc * S + ts_i * 128:
                                            dc * S + (ts_i + 1) * 128], pb[:, :]),
                        sem="V")
            tok_y1_spill = P.cnt["Y"]
            end_p2 = P.snap()
            close_phase(p2)

            if out_stage == "y1":
                pd = new_phase()
                dbg = pd.enter_context(nc.sbuf_tensor("dbg", [128, D], F32))
                tok_prev = 0
                for ts_i in range(T):
                    P.wait("gpsimd", "Y", tok_y1_spill)
                    P.wait("gpsimd", "V", tok_prev)
                    tok_l = P.dma("gpsimd", dbg[:, :],
                                  y1_scr[ts_i * 128:(ts_i + 1) * 128, :], "G")
                    P.wait("vector", "G", tok_l)
                    tok_prev = P.op("vector", lambda Eg: Eg.tensor_scalar(
                        dbg[:, :], dbg[:, :], 0.5, None, ALU.mult),
                        sem="V")
                    P.wait("gpsimd", "V", tok_prev)
                    P.dma("gpsimd", out_h[ts_i * 128:(ts_i + 1) * 128, :],
                          dbg[:, :], "G")
                close_phase(pd)
                raise _Stop


            P.tag("p3")
            CH6 = [[0, 1, 2], [3, 4, 5], [6, 7, 8], [9, 10, 11],
                   [12, 13, 14], [15]]
            p3 = new_phase()
            Kf = p3.enter_context(nc.sbuf_tensor("Kf", [128, 6 * S], BF16))
            Vf = p3.enter_context(nc.sbuf_tensor("Vf", [128, T * 528], BF16))
            qwsb = p3.enter_context(nc.sbuf_tensor("qwsb", [128, 8 * 512], BF16))
            kbT = p3.enter_context(nc.sbuf_tensor("kbT", [128, 6], F32))
            qbT = p3.enter_context(nc.sbuf_tensor("qbT", [128, 6], F32))
            vb1 = p3.enter_context(nc.sbuf_tensor("vb1", [1, 512], F32))
            vbB = p3.enter_context(nc.sbuf_tensor("vbB", [128, 512], F32))
            Qf = p3.enter_context(nc.sbuf_tensor("Qf", [128, 512], BF16))
            Pbuf2 = p3.enter_context(nc.sbuf_tensor("Pbuf2", [128, 2 * 1024], BF16))
            avs2_b = [p3.enter_context(nc.sbuf_tensor(f"avs2{i}", [32, 512], F32))
                      for i in range(3)]
            rcp2_b = [p3.enter_context(nc.sbuf_tensor(f"rcp2{i}", [1, 512], F32))
                      for i in range(3)]
            rcp2bf = [p3.enter_context(nc.sbuf_tensor(f"rcp2bf{i}", [1, 512], BF16))
                      for i in range(3)]
            AOq = p3.enter_context(nc.sbuf_tensor("AOq", [128, S], BF16))
            ps_p = p3.enter_context(nc.psum_tensor("ps_p", [128, 512], F32))
            ps_sc2 = [p3.enter_context(nc.psum_tensor("ps_sc2_0", [128, 2048], F32))]
            ps_av2v = [p3.enter_context(nc.psum_tensor(f"ps_av2{i}", [33, 512], F32))
                       for i in range(2)]
            ps_heat = p3.enter_context(nc.psum_tensor("ps_heat", [128, 512], F32))
            p3w = new_phase()
            kwsb = p3w.enter_context(nc.sbuf_tensor("kwsb", [128, 8 * 512], BF16))
            vwsb = p3w.enter_context(nc.sbuf_tensor("vwsb", [128, 8 * 512], BF16))

            bar2 = end_p2
            P.wait_all("gpsimd", bar2)
            P.wait_all("sync", bar2)
            tok_w3 = 0
            for dc in range(8):
                P.dma("gpsimd", kwsb[:, dc * 512:(dc + 1) * 512],
                      kw_h[dc * 128:(dc + 1) * 128, :], "G")
                P.dma("gpsimd", vwsb[:, dc * 512:(dc + 1) * 512],
                      vw_h[dc * 128:(dc + 1) * 128, :], "G")
                tok_w3 = P.dma("gpsimd", qwsb[:, dc * 512:(dc + 1) * 512],
                               qw32_h[dc * 128:(dc + 1) * 128, :], "G")
            P.dma("sync", kbT[:, :], kbT_h[:, :], "Y")
            P.dma("sync", qbT[:, :], qb32T_h[:, :], "Y")
            tok_b3 = P.dma("sync", vb1[:, :], vb_h[:, :], "Y")
            P.wait("tensor", "Y", tok_b3)
            P.wait("tensor", "V", end_p2["V"])
            tok_vbb = P.op("tensor", lambda Eg: Eg.matmul(
                ps_p[:, :], one1f[:, :], vb1[:, :], start=True, stop=True,
                skip_group_check=True), sem="T")
            P.wait("vector", "T", tok_vbb)
            tok_vbb_v = P.op("vector", lambda Eg:
                             Eg.tensor_copy(vbB[:, :], ps_p[:, :]), sem="V")
            tok_y1fm_all = max(y1fm_evac)
            tok_vones = P.op("vector", lambda Eg: Eg.memset(Vf[:, :], 1.0),
                             sem="V")
            pp_war = tok_vbb_v
            for c6, heads6 in enumerate(CH6):
                w6 = 32 * len(heads6)
                for tb in range(NB):
                    mms = []
                    for dc in range(8):
                        mms.append(lambda Eg, dc=dc, c6=c6, w6=w6, tb=tb:
                                   Eg.matmul(
                                       ps_p[0:w6, :],
                                       kwsb[:, dc * 512 + c6 * 96:
                                            dc * 512 + c6 * 96 + w6],
                                       y1fm[:, dc * S + tb * 512:
                                            dc * S + (tb + 1) * 512],
                                       start=(dc == 0), stop=(dc == 7)))
                    if c6 == 0 and tb == 0:
                        P.wait("tensor", "G", tok_w3)
                        P.wait("tensor", "V", tok_y1fm_all)
                        P.wait("tensor", "Y", P.cnt["Y"])
                    P.wait("tensor", "V", pp_war)
                    tok_k3 = P.mm_group(mms)
                    P.wait("vector", "T", tok_k3)
                    pp_war = P.op("vector", lambda Eg, c6=c6, w6=w6, tb=tb:
                                  Eg.tensor_scalar(
                                      Kf[0:w6, c6 * S + tb * 512:
                                         c6 * S + (tb + 1) * 512],
                                      ps_p[0:w6, :], kbT[0:w6, c6:c6 + 1],
                                      None, ALU.add), sem="V")
            tok_Kf = pp_war
            for tt_i in range(T):
                mms = []
                for dc in range(8):
                    mms.append(lambda Eg, dc=dc, tt_i=tt_i: Eg.matmul(
                        ps_p[:, :],
                        y1fm[:, dc * S + tt_i * 128: dc * S + (tt_i + 1) * 128],
                        vwsb[:, dc * 512:(dc + 1) * 512],
                        start=(dc == 0), stop=(dc == 7)))
                P.wait("tensor", "V", pp_war)
                tok_v3 = P.mm_group(mms)
                P.wait("vector", "T", tok_v3)
                if tt_i == 0:
                    P.wait("vector", "V", tok_vones)
                dstv = Vf[:, tt_i * 528: tt_i * 528 + 528].rearrange(
                    "p (m c) -> p m c", m=16)
                srcv = ps_p[:, :].rearrange("p (m c) -> p m c", m=16)
                vbv = vbB[:, :].rearrange("p (m c) -> p m c", m=16)
                pp_war = P.op("vector", lambda Eg, dstv=dstv, srcv=srcv, vbv=vbv:
                              Eg.tensor_add(dstv[:, :, 0:32], srcv, vbv),
                              sem="V")
            tok_Vf = pp_war
            prod_end = P.snap()
            P.wait_all("gpsimd", {k: prod_end[k] for k in ("T", "V")})
            close_phase(p3w)

            P.tag("p4")
            sc2_war = [0, 0]
            pb2_war = [0, 0]
            av2_s = [0, 0]
            av2_v = [0, 0]
            av2_idx = [0]
            bc2_war = 0
            qf_last_score = 0
            aoq_fill = 0
            aoq_dma = 0
            for c6, heads6 in enumerate(CH6):
                w6 = 32 * len(heads6)
                for sb in range(NB):
                    mms = []
                    for dc in range(8):
                        mms.append(lambda Eg, dc=dc, c6=c6, w6=w6, sb=sb:
                                   Eg.matmul(
                                       ps_p[0:w6, :],
                                       qwsb[:, dc * 512 + c6 * 96:
                                            dc * 512 + c6 * 96 + w6],
                                       y1fm[:, dc * S + sb * 512:
                                            dc * S + (sb + 1) * 512],
                                       start=(dc == 0), stop=(dc == 7)))
                    P.wait("tensor", "V", pp_war)
                    P.wait("tensor", "V", bc2_war)
                    tok_q4 = P.mm_group(mms)
                    P.wait("vector", "T", tok_q4)
                    P.wait("vector", "T", qf_last_score)
                    pp_war = P.op("vector", lambda Eg, c6=c6, w6=w6:
                                  Eg.tensor_scalar(Qf[0:w6, :], ps_p[0:w6, :],
                                                   qbT[0:w6, c6:c6 + 1], None,
                                                   ALU.add), sem="V")
                    tok_Qf = pp_war
                    for hh in range(len(heads6)):
                        h_glob = heads6[hh]
                        ksub = hh * 32
                        batches = [(b, [b, b + 1]) for b in range(0, T, 2)]

                        def f_scores(bi):
                            tcb, tcs = batches[bi]
                            buf_i = bi % 2
                            P.wait("tensor", "V", tok_Qf)
                            P.wait("tensor", "V", tok_Kf)
                            P.wait("tensor", "S", sc2_war[buf_i % len(sc2_war)])
                            mms = []
                            for j, tc in enumerate(tcs):
                                mms.append(lambda Eg, tc=tc, j=j, c6=c6,
                                           ksub=ksub, buf_i=buf_i: Eg.matmul(
                                               ps_sc2[0][:, (buf_i * 2 + j) * 512:
                                                         (buf_i * 2 + j + 1) * 512],
                                               Kf[ksub:ksub + 32,
                                                  c6 * S + tc * 128:
                                                  c6 * S + (tc + 1) * 128],
                                               Qf[ksub:ksub + 32, :],
                                               start=True, stop=True,
                                               skip_group_check=True))
                            tok_sc = P.mm_group(mms)
                            P.op("tensor", lambda Eg: Eg.matmul(
                                ps_heat[:, :], y1fm[:, 0:128], y1fm[:, 0:512],
                                start=True, stop=True, skip_group_check=True))
                            P.wait("scalar", "T", tok_sc)
                            P.wait("scalar", "T", pb2_war[buf_i])
                            tok_e = P.op("scalar", lambda Eg, buf_i=buf_i:
                                         Eg.activation(
                                             Pbuf2[:, buf_i * 1024:
                                                   buf_i * 1024 + 1024],
                                             ps_sc2[0][:, buf_i * 1024:
                                                       buf_i * 1024 + 1024],
                                             AF.Exp,
                                             bias=SHIFT_FFN, scale=SCALE_FFN),
                                         sem="S")
                            sc2_war[buf_i % len(sc2_war)] = tok_e
                            return (tok_e, buf_i)

                        def f_av(bi, meta, first):
                            tcb, tcs = batches[bi]
                            tok_e, buf_i = meta
                            pav2 = ps_av2v[av2_idx[0] % 2]
                            P.wait("tensor", "S", tok_e)
                            if first:
                                P.wait("tensor", "V", tok_Vf)
                                P.wait("tensor", "S", av2_s[av2_idx[0] % 2])
                                P.wait("tensor", "V", av2_v[av2_idx[0] % 2])
                            mms = []
                            for j, tc in enumerate(tcs):
                                mms.append(lambda Eg, tc=tc, j=j,
                                           h_glob=h_glob, buf_i=buf_i,
                                           pav2=pav2,
                                           first=(first and j == 0): Eg.matmul(
                                               pav2[:, :],
                                               Vf[:, tc * 528 + h_glob * 33:
                                                  tc * 528 + (h_glob + 1) * 33],
                                               Pbuf2[:, buf_i * 1024 + j * 512:
                                                     buf_i * 1024 + (j + 1) * 512],
                                               start=first, stop=(tc == T - 1),
                                               skip_group_check=True))
                            tok = P.mm_group(mms)
                            pb2_war[buf_i] = tok
                            return tok

                        metas = {}
                        tok_av = None
                        for bi in range(len(batches)):
                            metas[bi] = f_scores(bi)
                            if bi >= 1:
                                tok_av = f_av(bi - 1, metas[bi - 1], bi == 1)
                        tok_av = f_av(len(batches) - 1, metas[len(batches) - 1],
                                      len(batches) == 1)
                        qf_last_score = tok_av
                        pi2 = av2_idx[0] % 2
                        av2_idx[0] += 1
                        pav2 = ps_av2v[pi2]
                        avs2 = avs2_b[pi2]
                        rcp2 = rcp2_b[pi2]
                        rbf2 = rcp2bf[pi2]
                        P.wait("vector", "T", tok_av)
                        t_rc2 = P.op("vector", lambda Eg, rcp2=rcp2, pav2=pav2:
                                     Eg.reciprocal(rcp2[:, :], pav2[32:33, :]),
                                     sem="V")
                        P.wait("vector", "V", t_rc2)
                        av2_v[pi2] = P.op(
                            "vector", lambda Eg, rbf2=rbf2, rcp2=rcp2:
                            Eg.tensor_copy(rbf2[:, :], rcp2[:, :]), sem="V")
                        P.wait("scalar", "T", tok_av)
                        P.wait("scalar", "V", bc2_war)
                        av2_s[pi2] = P.op(
                            "scalar", lambda Eg, avs2=avs2, pav2=pav2:
                            Eg.copy(avs2[:, :], pav2[0:32, :]), sem="S")
                        P.wait("tensor", "V", av2_v[pi2])
                        P.wait("tensor", "V", bc2_war)
                        P.wait("tensor", "V", pp_war)
                        tok_bc = P.op("tensor", lambda Eg, rbf2=rbf2: Eg.matmul(
                            ps_p[0:32, :], onebs[:, 0:32], rbf2[:, :],
                            start=True, stop=True, skip_group_check=True),
                            sem="T")
                        P.wait("vector", "T", tok_bc)
                        P.wait("vector", "S", av2_s[pi2])
                        P.wait("vector", "G", aoq_dma)
                        bc2_war = P.op("vector", lambda Eg, hh=hh, sb=sb,
                                       avs2=avs2:
                                       Eg.tensor_mul(
                                           AOq[hh * 32:(hh + 1) * 32,
                                               sb * ST:(sb + 1) * ST],
                                           avs2[:, :], ps_p[0:32, :]), sem="V")
                        aoq_fill = bc2_war
                P.wait("gpsimd", "V", aoq_fill)
                aoq_dma = P.dma("gpsimd",
                                aof_scr[c6 * 96:c6 * 96 + w6, :],
                                AOq[0:w6, :], "G")
            tok_aof_spill = P.cnt["G"]
            end_p4 = P.snap()
            if out_stage == "kf":
                dump_and_stop([(Kf[:, ch * S:ch * S + S], ch * 128, S)
                               for ch in range(6)])
            if out_stage == "aofd":
                dump_and_stop([(AOq[:, 0:S], 0, S)])
            close_phase(p3)
            close_phase(pB)

            P.tag("p5")
            pC = new_phase()
            mla = pC.enter_context(nc.sbuf_tensor("mla", [128, 8 * S], BF16))
            c_all = pC.enter_context(nc.sbuf_tensor("c_all", [128, T * 8], F32))
            b1Ts = pC.enter_context(nc.sbuf_tensor("b1Ts", [128, 128], F32))
            b2sb = pC.enter_context(nc.sbuf_tensor("b2sb", [1, 8 * D], BF16))
            p5 = new_phase()
            aof_all = p5.enter_context(nc.sbuf_tensor("aof_all", [128, 4 * S], BF16))
            owsb = p5.enter_context(nc.sbuf_tensor("owsb", [128, 4 * D], BF16))
            rwsb = p5.enter_context(nc.sbuf_tensor("rwsb", [128, 8 * 8], BF16))
            rbBs = p5.enter_context(nc.sbuf_tensor("rbBs", [128, 8], F32))
            obTs = p5.enter_context(nc.sbuf_tensor("obTs", [128, 8], F32))
            rst = p5.enter_context(nc.sbuf_tensor("rst", [128, 64], F32))
            ps_o5 = [p5.enter_context(nc.psum_tensor(f"ps_o5{i}", [128, 512], F32))
                     for i in range(2)]
            ps_r = p5.enter_context(nc.psum_tensor("ps_r", [128, 8], F32))

            bar4 = end_p4
            P.wait_all("gpsimd", bar4)
            P.wait_all("sync", bar4)
            tok_aof = 0
            for ac in range(4):
                if ac == 0:
                    P.wait("gpsimd", "G", tok_aof_spill)
                tok_aof = P.dma("gpsimd", aof_all[:, ac * S:(ac + 1) * S],
                                aof_scr[ac * 128:(ac + 1) * 128, :], "G")
            if out_stage == "aofall":
                dump_and_stop([(aof_all[:, ac * S:ac * S + S], ac * 128, S)
                               for ac in range(4)])
            tok_w5 = 0
            for ac in range(4):
                tok_w5 = P.dma("gpsimd", owsb[:, ac * D:(ac + 1) * D],
                               ow_h[ac * 128:(ac + 1) * 128, :], "G")
            for dc in range(8):
                tok_w5 = P.dma("gpsimd", rwsb[:, dc * 8:(dc + 1) * 8],
                               rw_h[dc * 128:(dc + 1) * 128, :], "G")
            for e in range(E):
                tok_w5 = P.dma("gpsimd", b2sb[:, e * D:(e + 1) * D],
                               b2_h[e:e + 1, :], "G")
            P.dma("sync", rbBs[:, :], rbB_h[:, :], "Y")
            P.dma("sync", obTs[:, :], obT_h[:, :], "Y")
            tok_b5 = P.dma("sync", b1Ts[:, :], b1T_h[:, :], "Y")

            o5_war = [0, 0]
            for dc in range(8):
                for sb in range(NB):
                    bi5 = (dc * NB + sb) % 2
                    pso = ps_o5[bi5]
                    mms = []
                    for ac in range(4):
                        mms.append(lambda Eg, ac=ac, dc=dc, sb=sb, pso=pso:
                                   Eg.matmul(
                                       pso[:, :],
                                       owsb[:, ac * D + dc * 128:
                                            ac * D + (dc + 1) * 128],
                                       aof_all[:, ac * S + sb * 512:
                                               ac * S + (sb + 1) * 512],
                                       start=(ac == 0), stop=(ac == 3)))
                    if dc == 0 and sb == 0:
                        P.wait("tensor", "G", tok_aof)
                        P.wait("tensor", "G", tok_w5)
                        P.wait("tensor", "Y", tok_b5)
                    P.wait("tensor", "V", o5_war[bi5])
                    tok_o5 = P.mm_group(mms)
                    P.wait("vector", "T", tok_o5)
                    o5_war[bi5] = P.op("vector", lambda Eg, dc=dc, sb=sb, pso=pso:
                                       Eg.tensor_scalar(
                                           mla[:, dc * S + sb * 512:
                                               dc * S + (sb + 1) * 512],
                                           pso[:, :], obTs[:, dc:dc + 1], None,
                                           ALU.add), sem="V")
            tok_mla = max(o5_war)
            if out_stage == "mla":
                dump_and_stop([(mla[:, dcc * S:dcc * S + S], dcc * 128, S)
                               for dcc in range(8)])

            def vch(f):
                P.wait("vector", "V", vst[0])
                vst[0] = P.op("vector", f, sem="V")
                return vst[0]

            vst = [0]
            rr_war = 0
            for ts_i in range(T):
                mms = []
                for dc in range(8):
                    mms.append(lambda Eg, dc=dc, ts_i=ts_i: Eg.matmul(
                        ps_r[:, :],
                        mla[:, dc * S + ts_i * 128: dc * S + (ts_i + 1) * 128],
                        rwsb[:, dc * 8:(dc + 1) * 8],
                        start=(dc == 0), stop=(dc == 7)))
                if ts_i == 0:
                    P.wait("tensor", "V", tok_mla)
                P.wait("tensor", "V", rr_war)
                tok_r5 = P.mm_group(mms)
                P.wait("vector", "T", tok_r5)
                lg = rst[:, 0:8]
                ex_ = rst[:, 8:16]
                pr = rst[:, 16:24]
                eq = rst[:, 24:32]
                pm = rst[:, 32:40]
                mk = rst[:, 40:48]
                s1 = rst[:, 48:49]
                nmx = rst[:, 49:50]
                zz = rst[:, 50:51]
                rz = rst[:, 51:52]
                s2 = rst[:, 52:53]
                dn = rst[:, 53:54]
                rd = rst[:, 54:55]
                rr_war = vch(lambda Eg: Eg.tensor_add(lg, ps_r[:, :], rbBs[:, :]))
                vch(lambda Eg: Eg.reduce_max(nmx, lg, axis=AX.X, negate=True))
                tok_nm = vst[0]
                P.wait("scalar", "V", tok_nm)
                tok_ex = P.op("scalar", lambda Eg, lg=lg, ex_=ex_, nmx=nmx:
                              Eg.activation(ex_, lg, AF.Exp, bias=nmx), sem="S")
                P.wait("vector", "S", tok_ex)
                vch(lambda Eg: Eg.reduce_sum(zz, ex_, axis=AX.X))
                vch(lambda Eg: Eg.reciprocal(rz, zz))
                vch(lambda Eg: Eg.tensor_scalar(pr, ex_, rz, None, ALU.mult))
                vch(lambda Eg: Eg.reduce_max(s1, pr, axis=AX.X))
                vch(lambda Eg: Eg.tensor_single_scalar(eq, pr, s1, ALU.is_equal))
                vch(lambda Eg: Eg.scalar_tensor_tensor(pm, eq, -2.0, pr,
                                                       ALU.mult, ALU.add))
                vch(lambda Eg: Eg.reduce_max(s2, pm, axis=AX.X))
                vch(lambda Eg: Eg.tensor_single_scalar(mk, pr, s2, ALU.is_ge))
                vch(lambda Eg: Eg.tensor_add(dn, s1, s2))
                vch(lambda Eg: Eg.tensor_scalar(dn, dn, 1e-8, None, ALU.add))
                vch(lambda Eg: Eg.reciprocal(rd, dn))
                vch(lambda Eg: Eg.tensor_scalar(pr, pr, rd, None, ALU.mult))
                vch(lambda Eg, ts_i=ts_i: Eg.tensor_mul(
                    c_all[:, ts_i * 8:(ts_i + 1) * 8], pr, mk))
                if out_stage == "rout0" and ts_i == 0:
                    dump_and_stop([(rst[:, :], 0, 64)])
            tok_call = vst[0]
            end_p5 = P.snap()
            close_phase(p5)
            if out_stage == "comb":
                dump_and_stop([(c_all[:, :], 0, T * 8)])

            P.tag("p6")
            p6 = new_phase()
            w1sb = p6.enter_context(nc.sbuf_tensor("w1sb", [128, 8 * FF], BF16))
            w2s = [p6.enter_context(nc.sbuf_tensor(f"w2s{i}", [128, 16 * D], BF16))
                   for i in range(2)]
            h1sb = p6.enter_context(nc.sbuf_tensor("h1sb", [128, 16 * 512], BF16))
            h2st = [p6.enter_context(nc.sbuf_tensor(f"h2st{i}", [128, D], F32))
                    for i in range(2)]
            ps_h1 = [p6.enter_context(nc.psum_tensor(f"ps_h1{i}", [128, 512], F32))
                     for i in range(2)]
            ps_h2 = [p6.enter_context(nc.psum_tensor(f"ps_h2{i}", [128, 1024], F32))
                     for i in range(2)]
            bar5 = end_p5
            P.wait_all("gpsimd", bar5)
            P.wait_all("sync", bar5)

            h1_rd = 0
            h2_war = [0, 0]
            st_dma = [0, 0]
            gel_war = [0, 0]
            h1_last = 0
            h2_prev = 0
            h2_prev2 = 0
            tok_w1l = [0] * (E + 1)
            tok_w2l = [0] * (E + 1)
            for dc in range(8):
                tok_w1l[0] = P.dma("gpsimd", w1sb[:, dc * FF:(dc + 1) * FF],
                                   w1_h[0, dc * 128:(dc + 1) * 128, :], "G")
            for fc in range(16):
                tok_w2l[0] = P.dma("gpsimd", w2s[0][:, fc * D:(fc + 1) * D],
                                   w2_h[0, fc * 128:(fc + 1) * 128, :], "G")
            for e in range(E):
                tok_w6 = max(tok_w1l[e], tok_w2l[e])
                if e + 1 < E:
                    P.wait("gpsimd", "T", h2_prev)
                    for fc in range(16):
                        tok_w2l[e + 1] = P.dma(
                            "gpsimd", w2s[(e + 1) % 2][:, fc * D:(fc + 1) * D],
                            w2_h[e + 1, fc * 128:(fc + 1) * 128, :], "G")
                for sb in range(NB):
                    for fc in range(16):
                        bi6 = fc % 2
                        ph = ps_h1[bi6]
                        mms = []
                        for dc in range(8):
                            mms.append(lambda Eg, dc=dc, fc=fc, sb=sb, ph=ph:
                                       Eg.matmul(
                                           ph[:, :],
                                           w1sb[:, dc * FF + fc * 128:
                                                dc * FF + (fc + 1) * 128],
                                           mla[:, dc * S + sb * 512:
                                               dc * S + (sb + 1) * 512],
                                           start=(dc == 0), stop=(dc == 7)))
                        if fc == 0 and sb == 0:
                            P.wait("tensor", "G", tok_w6)
                        if fc == 0 and sb == NB - 1 and e + 1 < E:
                            pass
                        if fc == 0:
                            P.wait("tensor", "T", h1_rd)
                        P.wait("tensor", "S", gel_war[bi6])
                        tok_h1 = P.mm_group(mms)
                        P.wait("scalar", "T", tok_h1)
                        gel_war[bi6] = P.op(
                            "scalar", lambda Eg, fc=fc, e=e, ph=ph:
                            Eg.activation(h1sb[:, fc * 512:(fc + 1) * 512],
                                          ph[:, :], AF.Gelu,
                                          bias=b1Ts[:, e * 16 + fc:
                                                    e * 16 + fc + 1]), sem="S")
                    tok_gelu_all = gel_war[15 % 2]
                    if sb == NB - 1 and e + 1 < E:
                        P.wait("gpsimd", "T", tok_h1)
                        for dc in range(8):
                            tok_w1l[e + 1] = P.dma(
                                "gpsimd", w1sb[:, dc * FF:(dc + 1) * FF],
                                w1_h[e + 1, dc * 128:(dc + 1) * 128, :], "G")
                    for st_i in range(4):
                        ts_abs = sb * 4 + st_i
                        bi6 = st_i % 2
                        ph2 = ps_h2[bi6]
                        mms = []
                        for half in range(2):
                            mms.append(lambda Eg, half=half, e=e, ph2=ph2:
                                       Eg.matmul(
                                           ph2[:, half * 512:(half + 1) * 512],
                                           onebs[:, :],
                                           b2sb[:, e * D + half * 512:
                                                e * D + (half + 1) * 512],
                                           start=True, stop=False,
                                           skip_group_check=True))
                            for fc in range(16):
                                mms.append(lambda Eg, fc=fc, half=half,
                                           st_i=st_i, ph2=ph2, e=e: Eg.matmul(
                                               ph2[:, half * 512:(half + 1) * 512],
                                               h1sb[:, fc * 512 + st_i * 128:
                                                    fc * 512 + (st_i + 1) * 128],
                                               w2s[e % 2][:, fc * D + half * 512:
                                                    fc * D + (half + 1) * 512],
                                               start=False, stop=(fc == 15),
                                               skip_group_check=True))
                        P.wait("tensor", "S", gel_war[0])
                        P.wait("tensor", "S", gel_war[1])
                        P.wait("tensor", "V", h2_war[bi6])
                        h1_rd = P.mm_group(mms)
                        P.wait("vector", "T", h1_rd)
                        P.wait("vector", "G", st_dma[bi6])
                        cslice = c_all[:, ts_abs * 8 + e: ts_abs * 8 + e + 1]
                        h2_war[bi6] = P.op(
                            "vector", lambda Eg, bi6=bi6, ph2=ph2, cslice=cslice:
                            Eg.tensor_scalar(h2st[bi6][:, :], ph2[:, :],
                                             cslice, None, ALU.mult), sem="V")
                        P.wait("gpsimd", "V", h2_war[bi6])
                        kwargs = {} if e == 0 else {"accum_op": ALU.add}
                        st_dma[bi6] = P.dma(
                            "gpsimd", moe_scr[ts_abs * 128:(ts_abs + 1) * 128, :],
                            h2st[bi6][:, :], "G", **kwargs)
                h2_prev = h1_rd
            tok_moe = P.cnt["G"]
            end_p6 = P.snap()
            close_phase(p6)
            close_phase(pC)

            P.tag("p7")
            p7 = new_phase()
            moet = [p7.enter_context(nc.sbuf_tensor(f"moet{i}", [128, D], F32))
                    for i in range(2)]
            y2t = [p7.enter_context(nc.sbuf_tensor(f"y2t{i}", [128, D], F32))
                   for i in range(2)]
            z2 = p7.enter_context(nc.sbuf_tensor("z2", [128, D], F32))
            z2sq = p7.enter_context(nc.sbuf_tensor("z2sq", [128, D], F32))
            st7 = p7.enter_context(nc.sbuf_tensor("st7", [128, 8], F32))
            ot = [p7.enter_context(nc.sbuf_tensor(f"ot{i}", [128, D], F32))
                  for i in range(2)]
            bar6 = end_p6
            P.wait_all("gpsimd", bar6)
            P.wait_all("sync", bar6)
            mo_war = [0, 0]
            yt_war = [0, 0]
            ot_war = [0, 0]
            z2_war = 0
            vst7 = [0]

            def vch7(f):
                P.wait("vector", "V", vst7[0])
                vst7[0] = P.op("vector", f, sem="V")
                return vst7[0]

            for ts_i in range(T):
                bi7 = ts_i % 2
                P.wait("gpsimd", "G", tok_moe)
                P.wait("gpsimd", "V", mo_war[bi7])
                tok_mo = P.dma("gpsimd", moet[bi7][:, :],
                               moe_scr[ts_i * 128:(ts_i + 1) * 128, :], "G")
                P.wait("sync", "V", yt_war[bi7])
                tok_yt = P.dma("sync", y2t[bi7][:, :],
                               y1_scr[ts_i * 128:(ts_i + 1) * 128, :], "Y")
                P.wait("vector", "G", tok_mo)
                P.wait("vector", "Y", tok_yt)
                if out_stage == "moe":
                    mo_war[bi7] = vch7(lambda Eg, bi7=bi7: Eg.tensor_copy(
                        ot[bi7][:, :], moet[bi7][:, :]))
                    yt_war[bi7] = mo_war[bi7]
                    P.wait("sync", "V", mo_war[bi7])
                    ot_war[bi7] = P.dma("sync",
                                        out_h[ts_i * 128:(ts_i + 1) * 128, :],
                                        ot[bi7][:, :], "Y")
                    continue
                mo_war[bi7] = vch7(lambda Eg, bi7=bi7: Eg.scalar_tensor_tensor(
                    z2[:, :], moet[bi7][:, :], 0.3, y2t[bi7][:, :],
                    ALU.mult, ALU.add))
                yt_war[bi7] = mo_war[bi7]
                z2_war = mo_war[bi7]
                t1 = vch7(lambda Eg: Eg.reduce_sum(st7[:, 0:1], z2[:, :],
                                                   axis=AX.X))
                P.wait("scalar", "V", z2_war)
                tok_sq = P.op("scalar", lambda Eg: Eg.square(z2sq[:, :],
                                                             z2[:, :]), sem="S")
                P.wait("vector", "S", tok_sq)
                vch7(lambda Eg: Eg.reduce_sum(st7[:, 1:2], z2sq[:, :], axis=AX.X))
                vch7(lambda Eg: Eg.tensor_scalar(st7[:, 2:3], st7[:, 0:1],
                                                 1.0 / D, None, ALU.mult))
                vch7(lambda Eg: Eg.tensor_scalar(st7[:, 3:4], st7[:, 1:2],
                                                 1.0 / D, None, ALU.mult))
                vch7(lambda Eg: Eg.tensor_mul(st7[:, 4:5], st7[:, 2:3],
                                              st7[:, 2:3]))
                tok_v7 = vch7(lambda Eg: Eg.tensor_sub(st7[:, 5:6], st7[:, 3:4],
                                                       st7[:, 4:5]))
                P.wait("scalar", "V", tok_v7)
                tok_ln7 = P.op("scalar", lambda Eg: Eg.activation(
                    st7[:, 6:7], st7[:, 5:6], AF.Ln, bias=np.float32(1e-5)),
                    sem="S")
                P.wait("scalar", "S", tok_ln7)
                tok_r7 = P.op("scalar", lambda Eg: Eg.activation(
                    st7[:, 7:8], st7[:, 6:7], AF.Exp, scale=np.float32(-0.5)),
                    sem="S")
                P.wait("vector", "S", tok_r7)
                vch7(lambda Eg: Eg.tensor_mul(st7[:, 0:1], st7[:, 2:3],
                                              st7[:, 7:8]))
                P.wait("vector", "Y", ot_war[bi7])
                tok_out = vch7(lambda Eg, bi7=bi7: Eg.tensor_scalar(
                    ot[bi7][:, :], z2[:, :], st7[:, 7:8], st7[:, 0:1],
                    ALU.mult, ALU.subtract))
                P.wait("sync", "V", tok_out)
                ot_war[bi7] = P.dma("sync",
                                    out_h[ts_i * 128:(ts_i + 1) * 128, :],
                                    ot[bi7][:, :], "Y")
            close_phase(p7)

            if out_stage not in ("full", "moe"):
                raise NotImplementedError(f"unknown stage {out_stage}")
        except _Stop:
            pass
        for st_ in reversed(phase_stacks):
            st_.close()
        P.op("gpsimd", lambda Eg: Eg.wait_ge(P.sems["G"], P.cnt["G"]))
        P.op("sync", lambda Eg: Eg.wait_ge(P.sems["Y"], P.cnt["Y"]))
        P.emit(en(nc.Block()))
    return nc


WNAMES = ["L0_attn_w_qkv", "L0_attn_w_out", "L0_ffn_q_w", "L0_ffn_q_b",
          "L0_ffn_k_w", "L0_ffn_k_b", "L0_ffn_v_w", "L0_ffn_v_b",
          "L0_ffn_o_w", "L0_ffn_o_b", "L0_ffn_router_w", "L0_ffn_router_b",
          "L0_ffn_w1", "L0_ffn_b1", "L0_ffn_w2", "L0_ffn_b2"]
DNAMES = ["w_qkv", "w_out", "fq_w", "fq_b", "fk_w", "fk_b", "fv_w", "fv_b",
          "fo_w", "fo_b", "fr_w", "fr_b", "fw1", "fb1", "fw2", "fb2"]


def layout_extras(inputs):
    qb = np.asarray(inputs["L0_ffn_q_b"], np.float32)
    kb = np.asarray(inputs["L0_ffn_k_b"], np.float32)
    ob = np.asarray(inputs["L0_ffn_o_b"], np.float32)
    b1 = np.asarray(inputs["L0_ffn_b1"], np.float32)
    rb = np.asarray(inputs["L0_ffn_router_b"], np.float32)
    qw = np.asarray(inputs["L0_ffn_q_w"], np.float32)
    cols32 = np.arange(1024).reshape(16, 64)[:, :32].reshape(-1)
    ex = {}
    ex["fq_w32"] = np.ascontiguousarray(qw[:, cols32])
    qb32 = qb[cols32]
    qbT6 = np.zeros((128, 6), np.float32)
    kbT6 = np.zeros((128, 6), np.float32)
    for c in range(6):
        w = min(96, 512 - c * 96)
        qbT6[:w, c] = qb32[c * 96:c * 96 + w]
        kbT6[:w, c] = kb[c * 96:c * 96 + w]
    ex["fq_b32T"] = qbT6
    ex["fk_bT"] = kbT6
    ex["fo_bT"] = np.ascontiguousarray(ob.reshape(8, 128).T)
    b1T = np.zeros((128, 128), np.float32)
    for e in range(8):
        b1T[:, e * 16:(e + 1) * 16] = b1[e].reshape(16, 128).T
    ex["fb1T"] = b1T
    ex["fr_bB"] = np.ascontiguousarray(np.tile(rb.reshape(1, 8), (128, 1)))
    ex["oneb"] = np.ones((1, 128), ml_dtypes.bfloat16)
    return ex


def make_in_maps(inputs, S=None):
    B = inputs["L0_attn_x"].shape[0]
    Sfull = inputs["L0_attn_x"].shape[1] if S is None else S
    consts = host_consts(Sfull)
    extras = layout_extras(inputs)
    in_maps = []
    for b in range(B):
        m = {"x": np.ascontiguousarray(inputs["L0_attn_x"][b][:Sfull])}
        for wn, dn in zip(WNAMES, DNAMES):
            a = np.ascontiguousarray(np.asarray(inputs[wn]))
            if a.ndim == 1:
                a = a.reshape(1, -1)
            m[dn] = a
        m.update(consts)
        m.update(extras)
        in_maps.append(m)
    return in_maps


def kernel(**inputs):
    S = inputs["L0_attn_x"].shape[1]
    B = inputs["L0_attn_x"].shape[0]
    nc = build_program(S)
    in_maps = make_in_maps(inputs)
    res = run_bass_kernel_spmd(nc, in_maps, core_ids=list(range(B)))
    return np.stack([r["out"] for r in res.results], axis=0)


# version 07
# speedup vs baseline: 1.0636x; 1.0636x over previous
import numpy as np
import ml_dtypes
from contextlib import ExitStack

import concourse.bass as bass
import concourse.mybir as mybir
from concourse.bass_utils import run_bass_kernel_spmd

F32 = mybir.dt.float32
BF16 = mybir.dt.bfloat16
AF = mybir.ActivationFunctionType
ALU = mybir.AluOpType
AX = mybir.AxisListType

D = 1024
H = 16
HD = 64
KPH = 32
E = 8
FF = 2048
SHIFT_ATTN = np.float32(-16.0)
SHIFT_FFN = np.float32(-16.0)
SCALE_ATTN = np.float32(1.0 / 8.0)
SCALE_FFN = np.float32(1.0 / np.sqrt(32.0))


def host_consts(S):
    T = S // 128
    half = HD // 2
    inv_freq = 1.0 / (10000.0 ** (np.arange(half, dtype=np.float64) / half))
    t = np.arange(S, dtype=np.float64)
    freqs = t[:, None] * inv_freq[None, :]
    cos = np.cos(freqs).astype(np.float32)
    sin = np.sin(freqs).astype(np.float32)
    cosT = np.zeros((128, 32 * T), np.float32)
    sinT = np.zeros((128, 32 * T), np.float32)
    for tt in range(T):
        cosT[:, tt * 32:(tt + 1) * 32] = cos[tt * 128:(tt + 1) * 128]
        sinT[:, tt * 32:(tt + 1) * 32] = sin[tt * 128:(tt + 1) * 128]
    masks = np.zeros((128, 4 * 512), ml_dtypes.bfloat16)
    tp = np.arange(128)[:, None]
    sp = np.arange(512)[None, :]
    for j in range(4):
        masks[:, j * 512:(j + 1) * 512] = ((j * 128 + tp) <= sp).astype(
            ml_dtypes.bfloat16)
    idn = np.eye(128).astype(ml_dtypes.bfloat16)
    one1f = np.ones((1, 128), np.float32)
    return {"cosT": cosT, "sinT": sinT, "masks": masks, "idn": idn,
            "one1f": one1f}


class Prog:
    ENGINES = ("tensor", "scalar", "vector", "gpsimd", "sync")

    def __init__(self, nc, ctx):
        self.nc = nc
        self.plan = {e: [] for e in self.ENGINES}
        self.sems = {}
        self.cnt = {}
        for name in ("T", "S", "V", "G", "Y"):
            self.sems[name] = ctx.enter_context(nc.semaphore("sem" + name))
            self.cnt[name] = 0

    def wait(self, eng, sem, tok):
        if tok is None or tok <= 0:
            return
        h = self.sems[sem]
        self.plan[eng].append(lambda Eg: Eg.wait_ge(h, tok))

    SERIALIZE = False
    SER_TAGS = set()
    ENG_SEM = {"tensor": "T", "scalar": "S", "vector": "V",
               "gpsimd": "G", "sync": "Y"}
    cur_tag = "init"

    def tag(self, t):
        self.cur_tag = t

    def op(self, eng, f, sem=None, inc=1):
        if self.SERIALIZE or (self.cur_tag in self.SER_TAGS):
            for sn in ("T", "S", "V", "G", "Y"):
                self.wait(eng, sn, self.cnt[sn])
        if sem is None:
            self.plan[eng].append(f)
            return None
        h = self.sems[sem]
        self.cnt[sem] += inc
        tok = self.cnt[sem]
        self.plan[eng].append(lambda Eg, f=f: f(Eg).then_inc(h, inc))
        return tok

    def dma(self, eng, out_ap, in_ap, sem, **kw):
        return self.op(eng, lambda Eg: Eg.dma_start(out_ap, in_ap, **kw),
                       sem=sem, inc=16)

    def mm_group(self, mms, sem="T"):
        for f in mms[:-1]:
            self.op("tensor", f)
        return self.op("tensor", mms[-1], sem=sem)

    def snap(self):
        return dict(self.cnt)

    def wait_all(self, eng, toks):
        for sname, tok in toks.items():
            self.wait(eng, sname, tok)

    def emit(self, block):
        for eng in self.ENGINES:
            seq = self.plan[eng]

            def body(Eg, seq=seq):
                for f in seq:
                    f(Eg)
            getattr(block, eng)(body)


class _Stop(Exception):
    pass


def build_program(S, out_stage="full"):
    nc = bass.Bass("TRN2", target_bir_lowering=False)
    T = S // 128
    NB = S // 512
    ST = 512

    x_h = nc.dram_tensor("x", [S, D], F32, kind="ExternalInput")
    wqkv_h = nc.dram_tensor("w_qkv", [3 * D, D], F32, kind="ExternalInput")
    wout_h = nc.dram_tensor("w_out", [D, D], F32, kind="ExternalInput")
    qw_h = nc.dram_tensor("fq_w", [D, D], F32, kind="ExternalInput")
    qb_h = nc.dram_tensor("fq_b", [1, D], F32, kind="ExternalInput")
    kw_h = nc.dram_tensor("fk_w", [D, 512], F32, kind="ExternalInput")
    kb_h = nc.dram_tensor("fk_b", [1, 512], F32, kind="ExternalInput")
    vw_h = nc.dram_tensor("fv_w", [D, 512], F32, kind="ExternalInput")
    vb_h = nc.dram_tensor("fv_b", [1, 512], F32, kind="ExternalInput")
    ow_h = nc.dram_tensor("fo_w", [D, D], F32, kind="ExternalInput")
    ob_h = nc.dram_tensor("fo_b", [1, D], F32, kind="ExternalInput")
    rw_h = nc.dram_tensor("fr_w", [D, E], F32, kind="ExternalInput")
    rb_h = nc.dram_tensor("fr_b", [1, E], F32, kind="ExternalInput")
    w1_h = nc.dram_tensor("fw1", [E, D, FF], F32, kind="ExternalInput")
    b1_h = nc.dram_tensor("fb1", [E, FF], F32, kind="ExternalInput")
    w2_h = nc.dram_tensor("fw2", [E, FF, D], F32, kind="ExternalInput")
    b2_h = nc.dram_tensor("fb2", [E, D], F32, kind="ExternalInput")
    cosT_h = nc.dram_tensor("cosT", [128, 32 * T], F32, kind="ExternalInput")
    sinT_h = nc.dram_tensor("sinT", [128, 32 * T], F32, kind="ExternalInput")
    masks_h = nc.dram_tensor("masks", [128, 4 * 512], BF16, kind="ExternalInput")
    idn_h = nc.dram_tensor("idn", [128, 128], BF16, kind="ExternalInput")
    one1f_h = nc.dram_tensor("one1f", [1, 128], F32, kind="ExternalInput")
    qw32_h = nc.dram_tensor("fq_w32", [D, 512], F32, kind="ExternalInput")
    qb32T_h = nc.dram_tensor("fq_b32T", [128, 6], F32, kind="ExternalInput")
    kbT_h = nc.dram_tensor("fk_bT", [128, 6], F32, kind="ExternalInput")
    obT_h = nc.dram_tensor("fo_bT", [128, 8], F32, kind="ExternalInput")
    b1T_h = nc.dram_tensor("fb1T", [128, 128], F32, kind="ExternalInput")
    rbB_h = nc.dram_tensor("fr_bB", [128, 8], F32, kind="ExternalInput")
    oneb_h = nc.dram_tensor("oneb", [1, 128], BF16, kind="ExternalInput")

    out_h = nc.dram_tensor("out", [S, D], F32, kind="ExternalOutput")

    ao_scr = nc.dram_tensor("ao_scr", [D, S], BF16, kind="Internal")
    aof_scr = nc.dram_tensor("aof_scr", [512, S], BF16, kind="Internal")
    y1_scr = nc.dram_tensor("y1_scr", [S, D], F32, kind="Internal")
    moe_scr = nc.dram_tensor("moe_scr", [S, D], F32, kind="Internal")

    with ExitStack() as ctx:
        en = ctx.enter_context
        P = Prog(nc, ctx)

        idn = en(nc.sbuf_tensor("idn_s", [128, 128], BF16))
        one1f = en(nc.sbuf_tensor("one1f_s", [1, 128], F32))
        onebs = en(nc.sbuf_tensor("onebs_s", [1, 128], BF16))
        for dst, src in ((idn, idn_h), (one1f, one1f_h), (onebs, oneb_h)):
            P.dma("sync", dst[:, :], src[:, :], "Y")
        tok_consts = P.cnt["Y"]


        phase_stacks = []

        def new_phase():
            st_ = ExitStack()
            phase_stacks.append(st_)
            return st_

        def close_phase(st_):
            st_.close()
            phase_stacks.remove(st_)

        def transpose_load(src_h, n_rt, dst_sb, dst_cols, bufs, psb, state,
                           dma_eng="gpsimd"):
            for r in range(n_rt):
                buf = bufs[r % 2]
                if state["ld"][r % 2]:
                    P.wait(dma_eng, "T", state["ld"][r % 2])
                tok_l = P.dma(dma_eng, buf[:, :],
                              src_h[r * 128:(r + 1) * 128, :], "G")
                last_t = None
                for dc in range(8):
                    pb = psb[dc % 2]
                    if dc == 0:
                        P.wait("tensor", "G", tok_l)
                    P.wait("tensor", "V", state["ev"][dc % 2])
                    last_t = P.op(
                        "tensor",
                        lambda Eg, buf=buf, dc=dc, pb=pb:
                        Eg.transpose(pb[:, :], buf[:, dc * 128:(dc + 1) * 128],
                                     idn[:, :]), sem="T")
                    P.wait("vector", "T", last_t)
                    state["ev"][dc % 2] = P.op(
                        "vector",
                        lambda Eg, pb=pb, dc=dc, r=r:
                        Eg.tensor_copy(
                            dst_sb[:, dc * dst_cols + r * 128:
                                   dc * dst_cols + (r + 1) * 128], pb[:, :]),
                        sem="V")
                state["ld"][r % 2] = last_t
            return max(state["ev"])

        def dump_and_stop(items):
            pdd = new_phase()
            dbgd = pdd.enter_context(nc.sbuf_tensor("dbgdump", [128, S], F32))
            snapd = P.snap()
            P.wait_all("vector", {k: snapd[k] for k in ("T", "S", "G", "Y")})
            prevd = 0
            for (srcd, row0, ncols) in items:
                npart = srcd.shape[0]
                P.wait("vector", "G", prevd)
                tokc = P.op("vector", lambda Eg, srcd=srcd, ncols=ncols,
                            npart=npart:
                            Eg.tensor_copy(dbgd[0:npart, 0:ncols], srcd),
                            sem="V")
                P.wait("gpsimd", "V", tokc)
                prevd = P.dma("gpsimd", out_h[row0:row0 + npart, 0:ncols],
                              dbgd[0:npart, 0:ncols], "G")
            raise _Stop


        try:
            pA = new_phase()
            cosT = pA.enter_context(nc.sbuf_tensor("cosT_s", [128, 32 * T], F32))
            sinT = pA.enter_context(nc.sbuf_tensor("sinT_s", [128, 32 * T], F32))
            masks = pA.enter_context(nc.sbuf_tensor("masks_s", [128, 4 * 512], BF16))
            for dst_c, src_c in ((cosT, cosT_h), (sinT, sinT_h),
                                 (masks, masks_h)):
                P.dma("sync", dst_c[:, :], src_c[:, :], "Y")
            tok_consts2 = P.cnt["Y"]
            P.wait("vector", "Y", tok_consts2)
            xfm = pA.enter_context(nc.sbuf_tensor("xfm", [128, 8 * S], BF16))
            wqkvT = pA.enter_context(nc.sbuf_tensor("wqkvT", [128, 8 * 3 * D], BF16))

            p0 = new_phase()
            ld0 = p0.enter_context(nc.sbuf_tensor("ld0", [128, D], BF16))
            ld1 = p0.enter_context(nc.sbuf_tensor("ld1", [128, D], BF16))
            pt0 = p0.enter_context(nc.psum_tensor("pt0", [128, 128], BF16))
            pt1 = p0.enter_context(nc.psum_tensor("pt1", [128, 128], BF16))
            st0 = {"ld": [0, 0], "ev": [0, 0]}
            P.wait("tensor", "Y", tok_consts)
            tok_xfm = transpose_load(x_h, T, xfm, S, (ld0, ld1), (pt0, pt1), st0)
            tok_wqkvT = transpose_load(wqkv_h, 24, wqkvT, 3 * D, (ld0, ld1),
                                       (pt0, pt1), st0)
            close_phase(p0)
            if out_stage == "xfm":
                dump_and_stop([(xfm[:, dc * S:dc * S + S], dc * 128, S)
                               for dc in range(8)])
            if out_stage == "wq":
                dump_and_stop([(wqkvT[:, dc * 3 * D:dc * 3 * D + D], dc * 128, D)
                               for dc in range(8)])

            p1 = new_phase()
            Kg = p1.enter_context(nc.sbuf_tensor("Kg", [128, 2 * S], BF16))
            Vg = p1.enter_context(nc.sbuf_tensor("Vg", [128, T * 260], BF16))
            Qg = p1.enter_context(nc.sbuf_tensor("Qg", [128, 2 * ST], BF16))
            Pbuf = p1.enter_context(nc.sbuf_tensor("Pbuf", [128, 2 * 2048], BF16))
            tmp1 = p1.enter_context(nc.sbuf_tensor("tmp1", [128, 256], F32))
            tmp2 = p1.enter_context(nc.sbuf_tensor("tmp2", [128, 256], F32))
            ktm = p1.enter_context(nc.sbuf_tensor("ktm", [128, 256], BF16))
            avs_b = [p1.enter_context(nc.sbuf_tensor(f"avs{i}", [64, 512], F32))
                     for i in range(2)]
            rcp_b = [p1.enter_context(nc.sbuf_tensor(f"rcp{i}", [1, 512], F32))
                     for i in range(2)]
            rcpbf = [p1.enter_context(nc.sbuf_tensor(f"rcpbf{i}", [1, 512], BF16))
                     for i in range(2)]
            AOp = [p1.enter_context(nc.sbuf_tensor(f"AOp{i}", [128, S], BF16))
                   for i in range(2)]
            ps_kv = p1.enter_context(nc.psum_tensor("ps_kv", [128, 512], F32))
            ps_t = [p1.enter_context(nc.psum_tensor("ps_t0", [128, 128], BF16))]
            ps_t = ps_t * 2
            ps_sc = [p1.enter_context(nc.psum_tensor("ps_sc0", [128, 2048], F32))]
            ps_avv = [p1.enter_context(nc.psum_tensor(f"ps_av{i}", [65, 512], F32))
                      for i in range(2)]

            def rope_tm(n_heads, hdim, ttile, ncols):
                halfd = hdim // 2
                psv = ps_kv[:, 0:ncols].rearrange("p (h d) -> p h d", h=n_heads)
                x1, x2 = psv[:, :, 0:halfd], psv[:, :, halfd:hdim]
                ov = ktm[:, 0:ncols].rearrange("p (h d) -> p h d", h=n_heads)
                o1, o2 = ov[:, :, 0:halfd], ov[:, :, halfd:hdim]
                c = cosT[:, ttile * 32:(ttile + 1) * 32].rearrange(
                    "p (o i) -> p o i", o=1)
                s = sinT[:, ttile * 32:(ttile + 1) * 32].rearrange(
                    "p (o i) -> p o i", o=1)
                t1v = tmp1[:, 0:ncols].rearrange("p (h d) -> p h d", h=n_heads)
                t1a, t1b = t1v[:, :, 0:halfd], t1v[:, :, halfd:hdim]
                t2v = tmp2[:, 0:ncols].rearrange("p (h d) -> p h d", h=n_heads)
                t2a, t2b = t2v[:, :, 0:halfd], t2v[:, :, halfd:hdim]

                def tt(Eg, o, a, b, op):
                    aa, bb = bass.broadcast_tensor_aps(a, b)
                    return Eg.tensor_tensor(o, aa, bb, op)
                P.op("vector", lambda Eg: tt(Eg, t1a, x1, c, ALU.mult))
                P.op("vector", lambda Eg: tt(Eg, t1b, x2, s, ALU.mult))
                P.op("vector", lambda Eg: tt(Eg, t2a, x1, s, ALU.mult))
                tm = P.op("vector", lambda Eg: tt(Eg, t2b, x2, c, ALU.mult),
                          sem="V")
                P.wait("vector", "V", tm)
                ts_ = P.op("vector", lambda Eg: Eg.tensor_sub(o1, t1a, t1b),
                           sem="V")
                P.wait("vector", "V", ts_)
                return P.op("vector", lambda Eg: Eg.tensor_add(o2, t2a, t2b),
                            sem="V")

            kv_free = 0
            kv_free_T = 0
            tp_war = [0, 0]
            sc_war = [0]
            pbuf_war = [0, 0]
            av_war_s = [0, 0]
            av_war_v = [0, 0]
            av_inst = [0]
            bc_war = 0
            ao_fill = [0, 0]
            aop_war = [0, 0]
            tok_Kg = 0
            tok_Vg = 0
            tok_Qg = 0
            prev_group_av = 0

            def produce_tm(tt_i, wcol0, out_to, n_out):
                mms = []
                for dc in range(8):
                    mms.append(lambda Eg, dc=dc, tt_i=tt_i, wcol0=wcol0, n_out=n_out:
                               Eg.matmul(ps_kv[:, 0:n_out],
                                         xfm[:, dc * S + tt_i * 128:
                                             dc * S + (tt_i + 1) * 128],
                                         wqkvT[:, dc * 3 * D + wcol0:
                                               dc * 3 * D + wcol0 + n_out],
                                         start=(dc == 0), stop=(dc == 7)))
                return P.mm_group(mms)

            for g in range(4):
                P.tag("kprod")
                for tt_i in range(T):
                    P.wait("tensor", "V", kv_free)
                    P.wait("tensor", "V", bc_war)
                    if g == 0 and tt_i == 0:
                        P.wait("tensor", "V", tok_xfm)
                        P.wait("tensor", "V", tok_wqkvT)
                    tokK = produce_tm(tt_i, D + g * 256, None, 256)
                    P.wait("vector", "T", tokK)
                    kv_free = rope_tm(4, HD, tt_i, 256)
                    for ch in range(2):
                        pb = ps_t[ch]
                        P.wait("tensor", "V", kv_free)
                        P.wait("tensor", "V", max(tp_war))
                        tok_tp = P.op("tensor", lambda Eg, ch=ch, pb=pb:
                                      Eg.transpose(pb[:, :],
                                                   ktm[:, ch * 128:(ch + 1) * 128],
                                                   idn[:, :]), sem="T")
                        P.wait("vector", "T", tok_tp)
                        tp_war[ch] = P.op(
                            "vector", lambda Eg, ch=ch, tt_i=tt_i, pb=pb:
                            Eg.tensor_copy(Kg[:, ch * S + tt_i * 128:
                                              ch * S + (tt_i + 1) * 128], pb[:, :]),
                            sem="V")
                tok_Kg = tp_war[1]
                if out_stage == "kg0" and g == 0:
                    dump_and_stop([(Kg[:, ch * S:ch * S + S], ch * 128, S)
                                   for ch in range(2)])
                P.tag("vprod")
                P.wait("vector", "T", prev_group_av)
                tok_vset = P.op("vector", lambda Eg: Eg.memset(Vg[:, :], 1.0),
                                sem="V")
                for tt_i in range(T):
                    P.wait("tensor", "V", kv_free)
                    P.wait("tensor", "V", bc_war)
                    tokV = produce_tm(tt_i, 2 * D + g * 256, None, 256)
                    P.wait("vector", "T", tokV)
                    if tt_i == 0:
                        P.wait("vector", "V", tok_vset)
                    for m6 in range(4):
                        kv_free = P.op(
                            "vector", lambda Eg, tt_i=tt_i, m6=m6:
                            Eg.tensor_copy(
                                Vg[:, tt_i * 260 + m6 * 65:
                                   tt_i * 260 + m6 * 65 + 64],
                                ps_kv[:, m6 * 64:(m6 + 1) * 64]),
                            sem="V" if m6 == 3 else None)
                tok_Vg = kv_free
                if out_stage == "vg0" and g == 0:
                    dump_and_stop([(Vg[:, tt * 260:tt * 260 + 260], tt * 128, 260)
                                   for tt in range(min(T, 8))])

                for sb in range(NB):
                    P.tag("qprod")
                    for st_i in range(4):
                        ts_abs = sb * 4 + st_i
                        P.wait("tensor", "V", kv_free)
                        P.wait("tensor", "V", bc_war)
                        tokQ = produce_tm(ts_abs, g * 256, None, 256)
                        P.wait("vector", "T", tokQ)
                        kv_free = rope_tm(4, HD, ts_abs, 256)
                        for ch in range(2):
                            pb = ps_t[ch]
                            P.wait("tensor", "V", kv_free)
                            P.wait("tensor", "V", max(tp_war))
                            tok_tp = P.op("tensor", lambda Eg, ch=ch, pb=pb:
                                          Eg.transpose(
                                              pb[:, :],
                                              ktm[:, ch * 128:(ch + 1) * 128],
                                              idn[:, :]), sem="T")
                            P.wait("vector", "T", tok_tp)
                            tp_war[ch] = P.op(
                                "vector", lambda Eg, ch=ch, st_i=st_i, pb=pb:
                                Eg.tensor_copy(Qg[:, ch * ST + st_i * 128:
                                                  ch * ST + (st_i + 1) * 128],
                                               pb[:, :]), sem="V")
                    tok_Qg = tp_war[1]
                    if out_stage == "qg0" and g == 0 and sb == 0:
                        dump_and_stop([(Qg[:, ch * ST:ch * ST + 512], ch * 128, 512)
                                       for ch in range(2)])

                    P.tag("attn")
                    ntc = (sb + 1) * 4
                    for pp in range(2):
                        kch = pp
                        batches = [(b, [b, b + 1]) for b in range(0, ntc, 2)]

                        def emit_scores(bi):
                            tcb, tcs = batches[bi]
                            P.wait("tensor", "V", tok_Qg)
                            P.wait("tensor", "V", tok_Kg)
                            P.wait("tensor", "S", sc_war[0])
                            mms = []
                            for j, tc in enumerate(tcs):
                                for m, ksub in enumerate((0, 64)):
                                    mms.append(
                                        lambda Eg, tc=tc, j=j, m=m, ksub=ksub,
                                        kch=kch:
                                        Eg.matmul(
                                            ps_sc[0][:, (m * 2 + j) * 512:
                                                     (m * 2 + j + 1) * 512],
                                            Kg[ksub:ksub + 64,
                                               kch * S + tc * 128:
                                               kch * S + (tc + 1) * 128],
                                            Qg[ksub:ksub + 64,
                                               kch * ST: kch * ST + 512],
                                            start=True, stop=True,
                                            skip_group_check=True))
                            tok_sc = P.mm_group(mms)
                            buf_i = bi % 2
                            P.wait("scalar", "T", tok_sc)
                            P.wait("scalar", "T", pbuf_war[buf_i])
                            tok_e = P.op(
                                "scalar", lambda Eg, buf_i=buf_i:
                                Eg.activation(
                                    Pbuf[:, buf_i * 2048: buf_i * 2048 + 2048],
                                    ps_sc[0][:, :], AF.Exp,
                                    bias=SHIFT_ATTN, scale=SCALE_ATTN), sem="S")
                            sc_war[0] = tok_e
                            mask_tok = None
                            for j, tc in enumerate(tcs):
                                if tc < sb * 4:
                                    continue
                                jj = tc - sb * 4
                                for m in range(2):
                                    P.wait("vector", "S", tok_e)
                                    mask_tok = P.op(
                                        "vector",
                                        lambda Eg, buf_i=buf_i, j=j, jj=jj, m=m:
                                        Eg.tensor_mul(
                                            Pbuf[:, buf_i * 2048 + (m * 2 + j) * 512:
                                                 buf_i * 2048 + (m * 2 + j + 1) * 512],
                                            Pbuf[:, buf_i * 2048 + (m * 2 + j) * 512:
                                                 buf_i * 2048 + (m * 2 + j + 1) * 512],
                                            masks[:, jj * 512:(jj + 1) * 512]),
                                        sem="V")
                            return (tok_e, mask_tok, buf_i)

                        def emit_av(bi, meta, first):
                            tcb, tcs = batches[bi]
                            tok_e, mask_tok, buf_i = meta
                            if mask_tok is not None:
                                P.wait("tensor", "V", mask_tok)
                            else:
                                P.wait("tensor", "S", tok_e)
                            if first:
                                P.wait("tensor", "V", tok_Vg)
                                for m in range(2):
                                    P.wait("tensor", "S", av_war_s[m])
                                    P.wait("tensor", "V", av_war_v[m])
                            mms = []
                            for j, tc in enumerate(tcs):
                                for m in range(2):
                                    hh = kch * 2 + m
                                    mms.append(
                                        lambda Eg, tc=tc, j=j, m=m, hh=hh,
                                        ntc=ntc, buf_i=buf_i,
                                        first=(first and j == 0):
                                        Eg.matmul(
                                            ps_avv[m][:, :],
                                            Vg[:, tc * 260 + hh * 65:
                                               tc * 260 + (hh + 1) * 65],
                                            Pbuf[:, buf_i * 2048 + (m * 2 + j) * 512:
                                                 buf_i * 2048 + (m * 2 + j + 1) * 512],
                                            start=first, stop=(tc == ntc - 1),
                                            skip_group_check=True))
                            tok = P.mm_group(mms)
                            pbuf_war[buf_i] = tok
                            return tok

                        metas = {}
                        tok_av = None
                        for bi in range(len(batches)):
                            metas[bi] = emit_scores(bi)
                            if bi >= 1:
                                tok_av = emit_av(bi - 1, metas[bi - 1], bi == 1)
                        tok_av = emit_av(len(batches) - 1, metas[len(batches) - 1],
                                         len(batches) == 1)
                        P.tag("fin")
                        for m in range(2):
                            hh = kch * 2 + m
                            pav = ps_avv[m]
                            avs = avs_b[m]
                            rcp = rcp_b[m]
                            rbf = rcpbf[m]
                            P.wait("vector", "T", tok_av)
                            t_rc = P.op("vector", lambda Eg, rcp=rcp, pav=pav:
                                        Eg.reciprocal(rcp[:, :], pav[64:65, :]),
                                        sem="V")
                            P.wait("vector", "V", t_rc)
                            av_war_v[m] = P.op(
                                "vector", lambda Eg, rbf=rbf, rcp=rcp:
                                Eg.tensor_copy(rbf[:, :], rcp[:, :]), sem="V")
                            P.wait("scalar", "T", tok_av)
                            P.wait("scalar", "V", bc_war)
                            av_war_s[m] = P.op(
                                "scalar", lambda Eg, avs=avs, pav=pav:
                                Eg.copy(avs[:, :], pav[0:64, :]), sem="S")
                            P.wait("tensor", "V", av_war_v[m])
                            P.wait("tensor", "V", bc_war)
                            P.wait("tensor", "V", kv_free)
                            tok_bc = P.op("tensor", lambda Eg, rbf=rbf:
                                          Eg.matmul(ps_kv[0:64, :],
                                                    onebs[:, 0:64],
                                                    rbf[:, :], start=True,
                                                    stop=True,
                                                    skip_group_check=True),
                                          sem="T")
                            kv_free_T = tok_bc
                            pair_i = hh // 2
                            dst = AOp[pair_i]
                            prow = (hh % 2) * 64
                            P.wait("vector", "T", tok_bc)
                            P.wait("vector", "S", av_war_s[m])
                            P.wait("vector", "G", aop_war[pair_i])
                            bc_war = P.op(
                                "vector", lambda Eg, dst=dst, prow=prow, sb=sb,
                                avs=avs:
                                Eg.tensor_mul(
                                    dst[prow:prow + 64, sb * ST:(sb + 1) * ST],
                                    avs[:, :], ps_kv[0:64, :]), sem="V")
                            ao_fill[pair_i] = bc_war
                            prev_group_av = tok_av
                P.tag("spill")
                for pi in range(2):
                    a_chunk = g * 2 + pi
                    P.wait("gpsimd", "V", ao_fill[pi])
                    aop_war[pi] = P.dma(
                        "gpsimd", ao_scr[a_chunk * 128:(a_chunk + 1) * 128, :],
                        AOp[pi][:, :], "G")
            tok_ao_spill = P.cnt["G"]
            end_p1 = P.snap()
            close_phase(p1)
            close_phase(pA)

            P.tag("p2")
            pB = new_phase()
            y1fm = pB.enter_context(nc.sbuf_tensor("y1fm", [128, 8 * S], BF16))
            p2 = new_phase()
            woutT = p2.enter_context(nc.sbuf_tensor("woutT", [128, 8 * D], BF16))
            ao_all = p2.enter_context(nc.sbuf_tensor("ao_all", [128, 8 * S], BF16))
            ld2 = [p2.enter_context(nc.sbuf_tensor(f"ld2_{i}", [128, D], BF16))
                   for i in range(2)]
            pt2 = [p2.enter_context(nc.psum_tensor(f"pt2_{i}", [128, 128], BF16))
                   for i in range(2)]
            P.wait_all("gpsimd", end_p1)
            P.wait_all("sync", end_p1)
            st2 = {"ld": [0, 0], "ev": [0, 0]}
            tok_woutT = transpose_load(wout_h, 8, woutT, D, (ld2[0], ld2[1]),
                                       (pt2[0], pt2[1]), st2)
            ao_toks = []
            for ac in range(8):
                if ac == 0:
                    P.wait("gpsimd", "G", tok_ao_spill)
                ao_toks.append(P.dma("gpsimd", ao_all[:, ac * S:(ac + 1) * S],
                                     ao_scr[ac * 128:(ac + 1) * 128, :], "G"))
            tok_ao_all = ao_toks[-1]
            if out_stage == "ao":
                dump_and_stop([(ao_all[:, ac * S:ac * S + S], ac * 128, S)
                               for ac in range(8)])

            xt = [p2.enter_context(nc.sbuf_tensor(f"xt{i}", [128, D], F32))
                  for i in range(2)]
            zt = p2.enter_context(nc.sbuf_tensor("zt", [128, D], F32))
            zsq = p2.enter_context(nc.sbuf_tensor("zsq", [128, D], F32))
            stt = p2.enter_context(nc.sbuf_tensor("stt", [128, 8], F32))
            y1t = p2.enter_context(nc.sbuf_tensor("y1t", [128, D], F32))
            y1b = p2.enter_context(nc.sbuf_tensor("y1b", [128, D], BF16))
            ps_o = p2.enter_context(nc.psum_tensor("ps_o", [128, 1024], F32))

            xt_war = [0, 0]
            ln_war = 0
            y1dma_war = 0
            y1b_war = 0
            y1t_war = 0
            y1fm_evac = [0, 0]
            for ts_i in range(T):
                P.tag("p2b")
                bi = ts_i % 2
                P.wait("sync", "V", xt_war[bi])
                tok_x = P.dma("sync", xt[bi][:, :],
                              x_h[ts_i * 128:(ts_i + 1) * 128, :], "Y")
                mms = []
                for half in range(2):
                    for ac in range(8):
                        mms.append(
                            lambda Eg, ac=ac, half=half, ts_i=ts_i:
                            Eg.matmul(ps_o[:, half * 512:(half + 1) * 512],
                                      ao_all[:, ac * S + ts_i * 128:
                                             ac * S + (ts_i + 1) * 128],
                                      woutT[:, ac * D + half * 512:
                                            ac * D + (half + 1) * 512],
                                      start=(ac == 0), stop=(ac == 7),
                                      skip_group_check=True))
                if ts_i == 0:
                    P.wait("tensor", "V", tok_woutT)
                    P.wait("tensor", "G", tok_ao_all)
                P.wait("tensor", "V", ln_war)
                tok_o = P.mm_group(mms)
                P.wait("vector", "T", tok_o)
                P.wait("vector", "Y", tok_x)
                P.wait("vector", "V", y1t_war)
                P.op("vector", lambda Eg, bi=bi:
                     Eg.tensor_add(zt[:, 0:512], ps_o[:, 0:512], xt[bi][:, 0:512]))
                ln_war = P.op("vector", lambda Eg, bi=bi:
                              Eg.tensor_add(zt[:, 512:1024], ps_o[:, 512:1024],
                                            xt[bi][:, 512:1024]), sem="V")
                xt_war[bi] = ln_war
                P.tag("p2c")
                P.wait("vector", "V", ln_war)
                t1 = P.op("vector", lambda Eg:
                          Eg.reduce_sum(stt[:, 0:1], zt[:, :], axis=AX.X),
                          sem="V")
                P.wait("scalar", "V", ln_war)
                tok_sq = P.op("scalar", lambda Eg: Eg.square(zsq[:, :], zt[:, :]),
                              sem="S")
                P.wait("vector", "S", tok_sq)
                t2 = P.op("vector", lambda Eg:
                          Eg.reduce_sum(stt[:, 1:2], zsq[:, :], axis=AX.X),
                          sem="V")
                P.wait("vector", "V", t1)
                t3 = P.op("vector", lambda Eg: Eg.tensor_scalar(
                    stt[:, 2:3], stt[:, 0:1], 1.0 / D, None, ALU.mult), sem="V")
                P.wait("vector", "V", t2)
                t4 = P.op("vector", lambda Eg: Eg.tensor_scalar(
                    stt[:, 3:4], stt[:, 1:2], 1.0 / D, None, ALU.mult), sem="V")
                P.wait("vector", "V", t3)
                t5 = P.op("vector", lambda Eg:
                          Eg.tensor_mul(stt[:, 4:5], stt[:, 2:3], stt[:, 2:3]),
                          sem="V")
                P.wait("vector", "V", t4)
                P.wait("vector", "V", t5)
                tok_v = P.op("vector", lambda Eg:
                             Eg.tensor_sub(stt[:, 5:6], stt[:, 3:4], stt[:, 4:5]),
                             sem="V")
                P.wait("scalar", "V", tok_v)
                tok_ln = P.op("scalar", lambda Eg: Eg.activation(
                    stt[:, 6:7], stt[:, 5:6], AF.Ln, bias=np.float32(1e-5)),
                    sem="S")
                P.wait("scalar", "S", tok_ln)
                tok_r = P.op("scalar", lambda Eg: Eg.activation(
                    stt[:, 7:8], stt[:, 6:7], AF.Exp, scale=np.float32(-0.5)),
                    sem="S")
                P.wait("vector", "S", tok_r)
                t6 = P.op("vector", lambda Eg:
                          Eg.tensor_mul(stt[:, 0:1], stt[:, 2:3], stt[:, 7:8]),
                          sem="V")
                P.wait("vector", "Y", y1dma_war)
                P.wait("vector", "V", t6)
                P.op("vector", lambda Eg: Eg.tensor_scalar(
                    y1t[:, :], zt[:, :], stt[:, 7:8], stt[:, 0:1],
                    ALU.mult, ALU.subtract))
                P.tag("p2d")
                y1b_war = P.op("vector", lambda Eg:
                               Eg.tensor_copy(y1b[:, :], y1t[:, :]), sem="V")
                y1t_war = P.op("vector", lambda Eg: Eg.tensor_scalar(
                    y1t[:, :], y1t[:, :], 2.0, None, ALU.mult),
                    sem="V")
                P.wait("sync", "V", y1t_war)
                y1dma_war = P.dma("sync",
                                  y1_scr[ts_i * 128:(ts_i + 1) * 128, :],
                                  y1t[:, :], "Y")
                for dc in range(8):
                    pbi = dc % 2
                    pb = pt2[pbi]
                    P.wait("tensor", "V", y1b_war)
                    P.wait("tensor", "V", y1fm_evac[pbi])
                    tok_tp = P.op("tensor", lambda Eg, dc=dc, pb=pb:
                                  Eg.transpose(pb[:, :],
                                               y1b[:, dc * 128:(dc + 1) * 128],
                                               idn[:, :]), sem="T")
                    P.wait("vector", "T", tok_tp)
                    y1fm_evac[pbi] = P.op(
                        "vector", lambda Eg, dc=dc, ts_i=ts_i, pb=pb:
                        Eg.tensor_copy(y1fm[:, dc * S + ts_i * 128:
                                            dc * S + (ts_i + 1) * 128], pb[:, :]),
                        sem="V")
            tok_y1_spill = P.cnt["Y"]
            end_p2 = P.snap()
            close_phase(p2)

            if out_stage == "y1":
                pd = new_phase()
                dbg = pd.enter_context(nc.sbuf_tensor("dbg", [128, D], F32))
                tok_prev = 0
                for ts_i in range(T):
                    P.wait("gpsimd", "Y", tok_y1_spill)
                    P.wait("gpsimd", "V", tok_prev)
                    tok_l = P.dma("gpsimd", dbg[:, :],
                                  y1_scr[ts_i * 128:(ts_i + 1) * 128, :], "G")
                    P.wait("vector", "G", tok_l)
                    tok_prev = P.op("vector", lambda Eg: Eg.tensor_scalar(
                        dbg[:, :], dbg[:, :], 0.5, None, ALU.mult),
                        sem="V")
                    P.wait("gpsimd", "V", tok_prev)
                    P.dma("gpsimd", out_h[ts_i * 128:(ts_i + 1) * 128, :],
                          dbg[:, :], "G")
                close_phase(pd)
                raise _Stop


            P.tag("p3")
            CH6 = [[0, 1, 2], [3, 4, 5], [6, 7, 8], [9, 10, 11],
                   [12, 13, 14], [15]]
            p3 = new_phase()
            Kf = p3.enter_context(nc.sbuf_tensor("Kf", [128, 6 * S], BF16))
            Vf = p3.enter_context(nc.sbuf_tensor("Vf", [128, T * 528], BF16))
            qwsb = p3.enter_context(nc.sbuf_tensor("qwsb", [128, 8 * 512], BF16))
            kbT = p3.enter_context(nc.sbuf_tensor("kbT", [128, 6], F32))
            qbT = p3.enter_context(nc.sbuf_tensor("qbT", [128, 6], F32))
            vb1 = p3.enter_context(nc.sbuf_tensor("vb1", [1, 512], F32))
            vbB = p3.enter_context(nc.sbuf_tensor("vbB", [128, 512], F32))
            Qf = p3.enter_context(nc.sbuf_tensor("Qf", [128, 512], BF16))
            Pbuf2 = p3.enter_context(nc.sbuf_tensor("Pbuf2", [128, 2 * 1024], BF16))
            avs2_b = [p3.enter_context(nc.sbuf_tensor(f"avs2{i}", [32, 512], F32))
                      for i in range(3)]
            rcp2_b = [p3.enter_context(nc.sbuf_tensor(f"rcp2{i}", [1, 512], F32))
                      for i in range(3)]
            rcp2bf = [p3.enter_context(nc.sbuf_tensor(f"rcp2bf{i}", [1, 512], BF16))
                      for i in range(3)]
            AOq = p3.enter_context(nc.sbuf_tensor("AOq", [128, S], BF16))
            ps_p = p3.enter_context(nc.psum_tensor("ps_p", [128, 512], F32))
            ps_sc2 = [p3.enter_context(nc.psum_tensor("ps_sc2_0", [128, 2048], F32))]
            ps_av2v = [p3.enter_context(nc.psum_tensor(f"ps_av2{i}", [33, 512], F32))
                       for i in range(3)]
            p3w = new_phase()
            kwsb = p3w.enter_context(nc.sbuf_tensor("kwsb", [128, 8 * 512], BF16))
            vwsb = p3w.enter_context(nc.sbuf_tensor("vwsb", [128, 8 * 512], BF16))

            bar2 = end_p2
            P.wait_all("gpsimd", bar2)
            P.wait_all("sync", bar2)
            tok_w3 = 0
            for dc in range(8):
                P.dma("gpsimd", kwsb[:, dc * 512:(dc + 1) * 512],
                      kw_h[dc * 128:(dc + 1) * 128, :], "G")
                P.dma("gpsimd", vwsb[:, dc * 512:(dc + 1) * 512],
                      vw_h[dc * 128:(dc + 1) * 128, :], "G")
                tok_w3 = P.dma("gpsimd", qwsb[:, dc * 512:(dc + 1) * 512],
                               qw32_h[dc * 128:(dc + 1) * 128, :], "G")
            P.dma("sync", kbT[:, :], kbT_h[:, :], "Y")
            P.dma("sync", qbT[:, :], qb32T_h[:, :], "Y")
            tok_b3 = P.dma("sync", vb1[:, :], vb_h[:, :], "Y")
            P.wait("tensor", "Y", tok_b3)
            P.wait("tensor", "V", end_p2["V"])
            tok_vbb = P.op("tensor", lambda Eg: Eg.matmul(
                ps_p[:, :], one1f[:, :], vb1[:, :], start=True, stop=True,
                skip_group_check=True), sem="T")
            P.wait("vector", "T", tok_vbb)
            tok_vbb_v = P.op("vector", lambda Eg:
                             Eg.tensor_copy(vbB[:, :], ps_p[:, :]), sem="V")
            tok_y1fm_all = max(y1fm_evac)
            tok_vones = P.op("vector", lambda Eg: Eg.memset(Vf[:, :], 1.0),
                             sem="V")
            pp_war = tok_vbb_v
            for c6, heads6 in enumerate(CH6):
                w6 = 32 * len(heads6)
                for tb in range(NB):
                    mms = []
                    for dc in range(8):
                        mms.append(lambda Eg, dc=dc, c6=c6, w6=w6, tb=tb:
                                   Eg.matmul(
                                       ps_p[0:w6, :],
                                       kwsb[:, dc * 512 + c6 * 96:
                                            dc * 512 + c6 * 96 + w6],
                                       y1fm[:, dc * S + tb * 512:
                                            dc * S + (tb + 1) * 512],
                                       start=(dc == 0), stop=(dc == 7)))
                    if c6 == 0 and tb == 0:
                        P.wait("tensor", "G", tok_w3)
                        P.wait("tensor", "V", tok_y1fm_all)
                        P.wait("tensor", "Y", P.cnt["Y"])
                    P.wait("tensor", "V", pp_war)
                    tok_k3 = P.mm_group(mms)
                    P.wait("vector", "T", tok_k3)
                    pp_war = P.op("vector", lambda Eg, c6=c6, w6=w6, tb=tb:
                                  Eg.tensor_scalar(
                                      Kf[0:w6, c6 * S + tb * 512:
                                         c6 * S + (tb + 1) * 512],
                                      ps_p[0:w6, :], kbT[0:w6, c6:c6 + 1],
                                      None, ALU.add), sem="V")
            tok_Kf = pp_war
            for tt_i in range(T):
                mms = []
                for dc in range(8):
                    mms.append(lambda Eg, dc=dc, tt_i=tt_i: Eg.matmul(
                        ps_p[:, :],
                        y1fm[:, dc * S + tt_i * 128: dc * S + (tt_i + 1) * 128],
                        vwsb[:, dc * 512:(dc + 1) * 512],
                        start=(dc == 0), stop=(dc == 7)))
                P.wait("tensor", "V", pp_war)
                tok_v3 = P.mm_group(mms)
                P.wait("vector", "T", tok_v3)
                if tt_i == 0:
                    P.wait("vector", "V", tok_vones)
                dstv = Vf[:, tt_i * 528: tt_i * 528 + 528].rearrange(
                    "p (m c) -> p m c", m=16)
                srcv = ps_p[:, :].rearrange("p (m c) -> p m c", m=16)
                vbv = vbB[:, :].rearrange("p (m c) -> p m c", m=16)
                pp_war = P.op("vector", lambda Eg, dstv=dstv, srcv=srcv, vbv=vbv:
                              Eg.tensor_add(dstv[:, :, 0:32], srcv, vbv),
                              sem="V")
            tok_Vf = pp_war
            prod_end = P.snap()
            P.wait_all("gpsimd", {k: prod_end[k] for k in ("T", "V")})
            close_phase(p3w)

            P.tag("p4")
            sc2_war = [0, 0]
            pb2_war = [0, 0]
            av2_s = [0, 0, 0]
            av2_v = [0, 0, 0]
            bc2_war = 0
            qf_last_score = 0
            aoq_fill = 0
            aoq_dma = 0
            for c6, heads6 in enumerate(CH6):
                w6 = 32 * len(heads6)
                for sb in range(NB):
                    mms = []
                    for dc in range(8):
                        mms.append(lambda Eg, dc=dc, c6=c6, w6=w6, sb=sb:
                                   Eg.matmul(
                                       ps_p[0:w6, :],
                                       qwsb[:, dc * 512 + c6 * 96:
                                            dc * 512 + c6 * 96 + w6],
                                       y1fm[:, dc * S + sb * 512:
                                            dc * S + (sb + 1) * 512],
                                       start=(dc == 0), stop=(dc == 7)))
                    P.wait("tensor", "V", pp_war)
                    P.wait("tensor", "V", bc2_war)
                    tok_q4 = P.mm_group(mms)
                    P.wait("vector", "T", tok_q4)
                    P.wait("vector", "T", qf_last_score)
                    pp_war = P.op("vector", lambda Eg, c6=c6, w6=w6:
                                  Eg.tensor_scalar(Qf[0:w6, :], ps_p[0:w6, :],
                                                   qbT[0:w6, c6:c6 + 1], None,
                                                   ALU.add), sem="V")
                    tok_Qf = pp_war
                    for hh in range(len(heads6)):
                        h_glob = heads6[hh]
                        ksub = hh * 32
                        batches = [(b, [b, b + 1]) for b in range(0, T, 2)]

                        def f_scores(bi):
                            tcb, tcs = batches[bi]
                            buf_i = bi % 2
                            P.wait("tensor", "V", tok_Qf)
                            P.wait("tensor", "V", tok_Kf)
                            P.wait("tensor", "S", sc2_war[buf_i % len(sc2_war)])
                            mms = []
                            for j, tc in enumerate(tcs):
                                mms.append(lambda Eg, tc=tc, j=j, c6=c6,
                                           ksub=ksub, buf_i=buf_i: Eg.matmul(
                                               ps_sc2[0][:, (buf_i * 2 + j) * 512:
                                                         (buf_i * 2 + j + 1) * 512],
                                               Kf[ksub:ksub + 32,
                                                  c6 * S + tc * 128:
                                                  c6 * S + (tc + 1) * 128],
                                               Qf[ksub:ksub + 32, :],
                                               start=True, stop=True,
                                               skip_group_check=True))
                            tok_sc = P.mm_group(mms)
                            P.wait("scalar", "T", tok_sc)
                            P.wait("scalar", "T", pb2_war[buf_i])
                            tok_e = P.op("scalar", lambda Eg, buf_i=buf_i:
                                         Eg.activation(
                                             Pbuf2[:, buf_i * 1024:
                                                   buf_i * 1024 + 1024],
                                             ps_sc2[0][:, buf_i * 1024:
                                                       buf_i * 1024 + 1024],
                                             AF.Exp,
                                             bias=SHIFT_FFN, scale=SCALE_FFN),
                                         sem="S")
                            sc2_war[buf_i % len(sc2_war)] = tok_e
                            return (tok_e, buf_i)

                        def f_av(bi, meta, first):
                            tcb, tcs = batches[bi]
                            tok_e, buf_i = meta
                            pav2 = ps_av2v[hh]
                            P.wait("tensor", "S", tok_e)
                            if first:
                                P.wait("tensor", "V", tok_Vf)
                                P.wait("tensor", "S", av2_s[hh])
                                P.wait("tensor", "V", av2_v[hh])
                            mms = []
                            for j, tc in enumerate(tcs):
                                mms.append(lambda Eg, tc=tc, j=j,
                                           h_glob=h_glob, buf_i=buf_i,
                                           pav2=pav2,
                                           first=(first and j == 0): Eg.matmul(
                                               pav2[:, :],
                                               Vf[:, tc * 528 + h_glob * 33:
                                                  tc * 528 + (h_glob + 1) * 33],
                                               Pbuf2[:, buf_i * 1024 + j * 512:
                                                     buf_i * 1024 + (j + 1) * 512],
                                               start=first, stop=(tc == T - 1),
                                               skip_group_check=True))
                            tok = P.mm_group(mms)
                            pb2_war[buf_i] = tok
                            return tok

                        metas = {}
                        tok_av = None
                        for bi in range(len(batches)):
                            metas[bi] = f_scores(bi)
                            if bi >= 1:
                                tok_av = f_av(bi - 1, metas[bi - 1], bi == 1)
                        tok_av = f_av(len(batches) - 1, metas[len(batches) - 1],
                                      len(batches) == 1)
                        qf_last_score = tok_av
                        pav2 = ps_av2v[hh]
                        avs2 = avs2_b[hh]
                        rcp2 = rcp2_b[hh]
                        rbf2 = rcp2bf[hh]
                        P.wait("vector", "T", tok_av)
                        t_rc2 = P.op("vector", lambda Eg, rcp2=rcp2, pav2=pav2:
                                     Eg.reciprocal(rcp2[:, :], pav2[32:33, :]),
                                     sem="V")
                        P.wait("vector", "V", t_rc2)
                        av2_v[hh] = P.op(
                            "vector", lambda Eg, rbf2=rbf2, rcp2=rcp2:
                            Eg.tensor_copy(rbf2[:, :], rcp2[:, :]), sem="V")
                        P.wait("scalar", "T", tok_av)
                        P.wait("scalar", "V", bc2_war)
                        av2_s[hh] = P.op(
                            "scalar", lambda Eg, avs2=avs2, pav2=pav2:
                            Eg.copy(avs2[:, :], pav2[0:32, :]), sem="S")
                        P.wait("tensor", "V", av2_v[hh])
                        P.wait("tensor", "V", bc2_war)
                        P.wait("tensor", "V", pp_war)
                        tok_bc = P.op("tensor", lambda Eg, rbf2=rbf2: Eg.matmul(
                            ps_p[0:32, :], onebs[:, 0:32], rbf2[:, :],
                            start=True, stop=True, skip_group_check=True),
                            sem="T")
                        P.wait("vector", "T", tok_bc)
                        P.wait("vector", "S", av2_s[hh])
                        P.wait("vector", "G", aoq_dma)
                        bc2_war = P.op("vector", lambda Eg, hh=hh, sb=sb,
                                       avs2=avs2:
                                       Eg.tensor_mul(
                                           AOq[hh * 32:(hh + 1) * 32,
                                               sb * ST:(sb + 1) * ST],
                                           avs2[:, :], ps_p[0:32, :]), sem="V")
                        aoq_fill = bc2_war
                P.wait("gpsimd", "V", aoq_fill)
                aoq_dma = P.dma("gpsimd",
                                aof_scr[c6 * 96:c6 * 96 + w6, :],
                                AOq[0:w6, :], "G")
            tok_aof_spill = P.cnt["G"]
            end_p4 = P.snap()
            if out_stage == "kf":
                dump_and_stop([(Kf[:, ch * S:ch * S + S], ch * 128, S)
                               for ch in range(6)])
            if out_stage == "aofd":
                dump_and_stop([(AOq[:, 0:S], 0, S)])
            close_phase(p3)
            close_phase(pB)

            P.tag("p5")
            pC = new_phase()
            mla = pC.enter_context(nc.sbuf_tensor("mla", [128, 8 * S], BF16))
            c_all = pC.enter_context(nc.sbuf_tensor("c_all", [128, T * 8], F32))
            b1Ts = pC.enter_context(nc.sbuf_tensor("b1Ts", [128, 128], F32))
            b2sb = pC.enter_context(nc.sbuf_tensor("b2sb", [1, 8 * D], BF16))
            p5 = new_phase()
            aof_all = p5.enter_context(nc.sbuf_tensor("aof_all", [128, 4 * S], BF16))
            owsb = p5.enter_context(nc.sbuf_tensor("owsb", [128, 4 * D], BF16))
            rwsb = p5.enter_context(nc.sbuf_tensor("rwsb", [128, 8 * 8], BF16))
            rbBs = p5.enter_context(nc.sbuf_tensor("rbBs", [128, 8], F32))
            obTs = p5.enter_context(nc.sbuf_tensor("obTs", [128, 8], F32))
            rst = p5.enter_context(nc.sbuf_tensor("rst", [128, 64], F32))
            ps_o5 = [p5.enter_context(nc.psum_tensor(f"ps_o5{i}", [128, 512], F32))
                     for i in range(2)]
            ps_r = p5.enter_context(nc.psum_tensor("ps_r", [128, 8], F32))

            bar4 = end_p4
            P.wait_all("gpsimd", bar4)
            P.wait_all("sync", bar4)
            tok_aof = 0
            for ac in range(4):
                if ac == 0:
                    P.wait("gpsimd", "G", tok_aof_spill)
                tok_aof = P.dma("gpsimd", aof_all[:, ac * S:(ac + 1) * S],
                                aof_scr[ac * 128:(ac + 1) * 128, :], "G")
            if out_stage == "aofall":
                dump_and_stop([(aof_all[:, ac * S:ac * S + S], ac * 128, S)
                               for ac in range(4)])
            tok_w5 = 0
            for ac in range(4):
                tok_w5 = P.dma("gpsimd", owsb[:, ac * D:(ac + 1) * D],
                               ow_h[ac * 128:(ac + 1) * 128, :], "G")
            for dc in range(8):
                tok_w5 = P.dma("gpsimd", rwsb[:, dc * 8:(dc + 1) * 8],
                               rw_h[dc * 128:(dc + 1) * 128, :], "G")
            for e in range(E):
                tok_w5 = P.dma("gpsimd", b2sb[:, e * D:(e + 1) * D],
                               b2_h[e:e + 1, :], "G")
            P.dma("sync", rbBs[:, :], rbB_h[:, :], "Y")
            P.dma("sync", obTs[:, :], obT_h[:, :], "Y")
            tok_b5 = P.dma("sync", b1Ts[:, :], b1T_h[:, :], "Y")

            o5_war = [0, 0]
            for dc in range(8):
                for sb in range(NB):
                    bi5 = (dc * NB + sb) % 2
                    pso = ps_o5[bi5]
                    mms = []
                    for ac in range(4):
                        mms.append(lambda Eg, ac=ac, dc=dc, sb=sb, pso=pso:
                                   Eg.matmul(
                                       pso[:, :],
                                       owsb[:, ac * D + dc * 128:
                                            ac * D + (dc + 1) * 128],
                                       aof_all[:, ac * S + sb * 512:
                                               ac * S + (sb + 1) * 512],
                                       start=(ac == 0), stop=(ac == 3)))
                    if dc == 0 and sb == 0:
                        P.wait("tensor", "G", tok_aof)
                        P.wait("tensor", "G", tok_w5)
                        P.wait("tensor", "Y", tok_b5)
                    P.wait("tensor", "V", o5_war[bi5])
                    tok_o5 = P.mm_group(mms)
                    P.wait("vector", "T", tok_o5)
                    o5_war[bi5] = P.op("vector", lambda Eg, dc=dc, sb=sb, pso=pso:
                                       Eg.tensor_scalar(
                                           mla[:, dc * S + sb * 512:
                                               dc * S + (sb + 1) * 512],
                                           pso[:, :], obTs[:, dc:dc + 1], None,
                                           ALU.add), sem="V")
            tok_mla = max(o5_war)
            if out_stage == "mla":
                dump_and_stop([(mla[:, dcc * S:dcc * S + S], dcc * 128, S)
                               for dcc in range(8)])

            def vch(f):
                P.wait("vector", "V", vst[0])
                vst[0] = P.op("vector", f, sem="V")
                return vst[0]

            vst = [0]
            rr_war = 0
            for ts_i in range(T):
                mms = []
                for dc in range(8):
                    mms.append(lambda Eg, dc=dc, ts_i=ts_i: Eg.matmul(
                        ps_r[:, :],
                        mla[:, dc * S + ts_i * 128: dc * S + (ts_i + 1) * 128],
                        rwsb[:, dc * 8:(dc + 1) * 8],
                        start=(dc == 0), stop=(dc == 7)))
                if ts_i == 0:
                    P.wait("tensor", "V", tok_mla)
                P.wait("tensor", "V", rr_war)
                tok_r5 = P.mm_group(mms)
                P.wait("vector", "T", tok_r5)
                lg = rst[:, 0:8]
                ex_ = rst[:, 8:16]
                pr = rst[:, 16:24]
                eq = rst[:, 24:32]
                pm = rst[:, 32:40]
                mk = rst[:, 40:48]
                s1 = rst[:, 48:49]
                nmx = rst[:, 49:50]
                zz = rst[:, 50:51]
                rz = rst[:, 51:52]
                s2 = rst[:, 52:53]
                dn = rst[:, 53:54]
                rd = rst[:, 54:55]
                rr_war = vch(lambda Eg: Eg.tensor_add(lg, ps_r[:, :], rbBs[:, :]))
                vch(lambda Eg: Eg.reduce_max(nmx, lg, axis=AX.X, negate=True))
                tok_nm = vst[0]
                P.wait("scalar", "V", tok_nm)
                tok_ex = P.op("scalar", lambda Eg, lg=lg, ex_=ex_, nmx=nmx:
                              Eg.activation(ex_, lg, AF.Exp, bias=nmx), sem="S")
                P.wait("vector", "S", tok_ex)
                vch(lambda Eg: Eg.reduce_sum(zz, ex_, axis=AX.X))
                vch(lambda Eg: Eg.reciprocal(rz, zz))
                vch(lambda Eg: Eg.tensor_scalar(pr, ex_, rz, None, ALU.mult))
                vch(lambda Eg: Eg.reduce_max(s1, pr, axis=AX.X))
                vch(lambda Eg: Eg.tensor_single_scalar(eq, pr, s1, ALU.is_equal))
                vch(lambda Eg: Eg.scalar_tensor_tensor(pm, eq, -2.0, pr,
                                                       ALU.mult, ALU.add))
                vch(lambda Eg: Eg.reduce_max(s2, pm, axis=AX.X))
                vch(lambda Eg: Eg.tensor_single_scalar(mk, pr, s2, ALU.is_ge))
                vch(lambda Eg: Eg.tensor_add(dn, s1, s2))
                vch(lambda Eg: Eg.tensor_scalar(dn, dn, 1e-8, None, ALU.add))
                vch(lambda Eg: Eg.reciprocal(rd, dn))
                vch(lambda Eg: Eg.tensor_scalar(pr, pr, rd, None, ALU.mult))
                vch(lambda Eg, ts_i=ts_i: Eg.tensor_mul(
                    c_all[:, ts_i * 8:(ts_i + 1) * 8], pr, mk))
                if out_stage == "rout0" and ts_i == 0:
                    dump_and_stop([(rst[:, :], 0, 64)])
            tok_call = vst[0]
            end_p5 = P.snap()
            close_phase(p5)
            if out_stage == "comb":
                dump_and_stop([(c_all[:, :], 0, T * 8)])

            P.tag("p6")
            p6 = new_phase()
            w1sb = p6.enter_context(nc.sbuf_tensor("w1sb", [128, 8 * FF], BF16))
            w2s = [p6.enter_context(nc.sbuf_tensor(f"w2s{i}", [128, 16 * D], BF16))
                   for i in range(2)]
            h1sb = p6.enter_context(nc.sbuf_tensor("h1sb", [128, 16 * 512], BF16))
            h2st = [p6.enter_context(nc.sbuf_tensor(f"h2st{i}", [128, D], F32))
                    for i in range(2)]
            ps_h1 = [p6.enter_context(nc.psum_tensor(f"ps_h1{i}", [128, 512], F32))
                     for i in range(2)]
            ps_h2 = [p6.enter_context(nc.psum_tensor(f"ps_h2{i}", [128, 1024], F32))
                     for i in range(2)]
            bar5 = end_p5
            P.wait_all("gpsimd", bar5)
            P.wait_all("sync", bar5)

            h1_rd = 0
            h2_war = [0, 0]
            st_dma = [0, 0]
            gel_war = [0, 0]
            h2_prev = 0
            tok_w1l = [0] * (E + 1)
            tok_w2l = [0] * (E + 1)
            for dc in range(8):
                tok_w1l[0] = P.dma("gpsimd", w1sb[:, dc * FF:(dc + 1) * FF],
                                   w1_h[0, dc * 128:(dc + 1) * 128, :], "G")
            for fc in range(16):
                tok_w2l[0] = P.dma("gpsimd", w2s[0][:, fc * D:(fc + 1) * D],
                                   w2_h[0, fc * 128:(fc + 1) * 128, :], "G")
            for e in range(E):
                tok_w6 = max(tok_w1l[e], tok_w2l[e])
                if e + 1 < E:
                    P.wait("gpsimd", "T", h2_prev)
                    for fc in range(16):
                        tok_w2l[e + 1] = P.dma(
                            "gpsimd", w2s[(e + 1) % 2][:, fc * D:(fc + 1) * D],
                            w2_h[e + 1, fc * 128:(fc + 1) * 128, :], "G")
                for sb in range(NB):
                    for fc in range(16):
                        bi6 = fc % 2
                        ph = ps_h1[bi6]
                        mms = []
                        for dc in range(8):
                            mms.append(lambda Eg, dc=dc, fc=fc, sb=sb, ph=ph:
                                       Eg.matmul(
                                           ph[:, :],
                                           w1sb[:, dc * FF + fc * 128:
                                                dc * FF + (fc + 1) * 128],
                                           mla[:, dc * S + sb * 512:
                                               dc * S + (sb + 1) * 512],
                                           start=(dc == 0), stop=(dc == 7)))
                        if fc == 0 and sb == 0:
                            P.wait("tensor", "G", tok_w6)
                        if fc == 0:
                            P.wait("tensor", "T", h1_rd)
                        P.wait("tensor", "S", gel_war[bi6])
                        tok_h1 = P.mm_group(mms)
                        P.wait("scalar", "T", tok_h1)
                        gel_war[bi6] = P.op(
                            "scalar", lambda Eg, fc=fc, e=e, ph=ph:
                            Eg.activation(h1sb[:, fc * 512:(fc + 1) * 512],
                                          ph[:, :], AF.Gelu,
                                          bias=b1Ts[:, e * 16 + fc:
                                                    e * 16 + fc + 1]), sem="S")
                    tok_gelu_all = gel_war[15 % 2]
                    if sb == NB - 1 and e + 1 < E:
                        P.wait("gpsimd", "T", tok_h1)
                        for dc in range(8):
                            tok_w1l[e + 1] = P.dma(
                                "gpsimd", w1sb[:, dc * FF:(dc + 1) * FF],
                                w1_h[e + 1, dc * 128:(dc + 1) * 128, :], "G")
                    for st_i in range(4):
                        ts_abs = sb * 4 + st_i
                        bi6 = st_i % 2
                        ph2 = ps_h2[bi6]
                        mms = []
                        for half in range(2):
                            mms.append(lambda Eg, half=half, e=e, ph2=ph2:
                                       Eg.matmul(
                                           ph2[:, half * 512:(half + 1) * 512],
                                           onebs[:, :],
                                           b2sb[:, e * D + half * 512:
                                                e * D + (half + 1) * 512],
                                           start=True, stop=False,
                                           skip_group_check=True))
                            for fc in range(16):
                                mms.append(lambda Eg, fc=fc, half=half,
                                           st_i=st_i, ph2=ph2, e=e: Eg.matmul(
                                               ph2[:, half * 512:(half + 1) * 512],
                                               h1sb[:, fc * 512 + st_i * 128:
                                                    fc * 512 + (st_i + 1) * 128],
                                               w2s[e % 2][:, fc * D + half * 512:
                                                    fc * D + (half + 1) * 512],
                                               start=False, stop=(fc == 15),
                                               skip_group_check=True))
                        P.wait("tensor", "S", gel_war[0])
                        P.wait("tensor", "S", gel_war[1])
                        P.wait("tensor", "V", h2_war[bi6])
                        h1_rd = P.mm_group(mms)
                        P.wait("vector", "T", h1_rd)
                        P.wait("vector", "G", st_dma[bi6])
                        cslice = c_all[:, ts_abs * 8 + e: ts_abs * 8 + e + 1]
                        h2_war[bi6] = P.op(
                            "vector", lambda Eg, bi6=bi6, ph2=ph2, cslice=cslice:
                            Eg.tensor_scalar(h2st[bi6][:, :], ph2[:, :],
                                             cslice, None, ALU.mult), sem="V")
                        P.wait("gpsimd", "V", h2_war[bi6])
                        kwargs = {} if e == 0 else {"accum_op": ALU.add}
                        st_dma[bi6] = P.dma(
                            "gpsimd", moe_scr[ts_abs * 128:(ts_abs + 1) * 128, :],
                            h2st[bi6][:, :], "G", **kwargs)
                h2_prev = h1_rd
            tok_moe = P.cnt["G"]
            end_p6 = P.snap()
            close_phase(p6)
            close_phase(pC)

            P.tag("p7")
            p7 = new_phase()
            moet = [p7.enter_context(nc.sbuf_tensor(f"moet{i}", [128, D], F32))
                    for i in range(2)]
            y2t = [p7.enter_context(nc.sbuf_tensor(f"y2t{i}", [128, D], F32))
                   for i in range(2)]
            z2 = p7.enter_context(nc.sbuf_tensor("z2", [128, D], F32))
            z2sq = p7.enter_context(nc.sbuf_tensor("z2sq", [128, D], F32))
            st7 = p7.enter_context(nc.sbuf_tensor("st7", [128, 8], F32))
            ot = [p7.enter_context(nc.sbuf_tensor(f"ot{i}", [128, D], F32))
                  for i in range(2)]
            bar6 = end_p6
            P.wait_all("gpsimd", bar6)
            P.wait_all("sync", bar6)
            mo_war = [0, 0]
            yt_war = [0, 0]
            ot_war = [0, 0]
            z2_war = 0
            vst7 = [0]

            def vch7(f):
                P.wait("vector", "V", vst7[0])
                vst7[0] = P.op("vector", f, sem="V")
                return vst7[0]

            for ts_i in range(T):
                bi7 = ts_i % 2
                P.wait("gpsimd", "G", tok_moe)
                P.wait("gpsimd", "V", mo_war[bi7])
                tok_mo = P.dma("gpsimd", moet[bi7][:, :],
                               moe_scr[ts_i * 128:(ts_i + 1) * 128, :], "G")
                P.wait("sync", "V", yt_war[bi7])
                tok_yt = P.dma("sync", y2t[bi7][:, :],
                               y1_scr[ts_i * 128:(ts_i + 1) * 128, :], "Y")
                P.wait("vector", "G", tok_mo)
                P.wait("vector", "Y", tok_yt)
                if out_stage == "moe":
                    mo_war[bi7] = vch7(lambda Eg, bi7=bi7: Eg.tensor_copy(
                        ot[bi7][:, :], moet[bi7][:, :]))
                    yt_war[bi7] = mo_war[bi7]
                    P.wait("sync", "V", mo_war[bi7])
                    ot_war[bi7] = P.dma("sync",
                                        out_h[ts_i * 128:(ts_i + 1) * 128, :],
                                        ot[bi7][:, :], "Y")
                    continue
                mo_war[bi7] = vch7(lambda Eg, bi7=bi7: Eg.scalar_tensor_tensor(
                    z2[:, :], moet[bi7][:, :], 0.3, y2t[bi7][:, :],
                    ALU.mult, ALU.add))
                yt_war[bi7] = mo_war[bi7]
                z2_war = mo_war[bi7]
                t1 = vch7(lambda Eg: Eg.reduce_sum(st7[:, 0:1], z2[:, :],
                                                   axis=AX.X))
                P.wait("scalar", "V", z2_war)
                tok_sq = P.op("scalar", lambda Eg: Eg.square(z2sq[:, :],
                                                             z2[:, :]), sem="S")
                P.wait("vector", "S", tok_sq)
                vch7(lambda Eg: Eg.reduce_sum(st7[:, 1:2], z2sq[:, :], axis=AX.X))
                vch7(lambda Eg: Eg.tensor_scalar(st7[:, 2:3], st7[:, 0:1],
                                                 1.0 / D, None, ALU.mult))
                vch7(lambda Eg: Eg.tensor_scalar(st7[:, 3:4], st7[:, 1:2],
                                                 1.0 / D, None, ALU.mult))
                vch7(lambda Eg: Eg.tensor_mul(st7[:, 4:5], st7[:, 2:3],
                                              st7[:, 2:3]))
                tok_v7 = vch7(lambda Eg: Eg.tensor_sub(st7[:, 5:6], st7[:, 3:4],
                                                       st7[:, 4:5]))
                P.wait("scalar", "V", tok_v7)
                tok_ln7 = P.op("scalar", lambda Eg: Eg.activation(
                    st7[:, 6:7], st7[:, 5:6], AF.Ln, bias=np.float32(1e-5)),
                    sem="S")
                P.wait("scalar", "S", tok_ln7)
                tok_r7 = P.op("scalar", lambda Eg: Eg.activation(
                    st7[:, 7:8], st7[:, 6:7], AF.Exp, scale=np.float32(-0.5)),
                    sem="S")
                P.wait("vector", "S", tok_r7)
                vch7(lambda Eg: Eg.tensor_mul(st7[:, 0:1], st7[:, 2:3],
                                              st7[:, 7:8]))
                P.wait("vector", "Y", ot_war[bi7])
                tok_out = vch7(lambda Eg, bi7=bi7: Eg.tensor_scalar(
                    ot[bi7][:, :], z2[:, :], st7[:, 7:8], st7[:, 0:1],
                    ALU.mult, ALU.subtract))
                P.wait("sync", "V", tok_out)
                ot_war[bi7] = P.dma("sync",
                                    out_h[ts_i * 128:(ts_i + 1) * 128, :],
                                    ot[bi7][:, :], "Y")
            close_phase(p7)

            if out_stage not in ("full", "moe"):
                raise NotImplementedError(f"unknown stage {out_stage}")
        except _Stop:
            pass
        for st_ in reversed(phase_stacks):
            st_.close()
        P.op("gpsimd", lambda Eg: Eg.wait_ge(P.sems["G"], P.cnt["G"]))
        P.op("sync", lambda Eg: Eg.wait_ge(P.sems["Y"], P.cnt["Y"]))
        P.emit(en(nc.Block()))
    return nc


WNAMES = ["L0_attn_w_qkv", "L0_attn_w_out", "L0_ffn_q_w", "L0_ffn_q_b",
          "L0_ffn_k_w", "L0_ffn_k_b", "L0_ffn_v_w", "L0_ffn_v_b",
          "L0_ffn_o_w", "L0_ffn_o_b", "L0_ffn_router_w", "L0_ffn_router_b",
          "L0_ffn_w1", "L0_ffn_b1", "L0_ffn_w2", "L0_ffn_b2"]
DNAMES = ["w_qkv", "w_out", "fq_w", "fq_b", "fk_w", "fk_b", "fv_w", "fv_b",
          "fo_w", "fo_b", "fr_w", "fr_b", "fw1", "fb1", "fw2", "fb2"]


def layout_extras(inputs):
    qb = np.asarray(inputs["L0_ffn_q_b"], np.float32)
    kb = np.asarray(inputs["L0_ffn_k_b"], np.float32)
    ob = np.asarray(inputs["L0_ffn_o_b"], np.float32)
    b1 = np.asarray(inputs["L0_ffn_b1"], np.float32)
    rb = np.asarray(inputs["L0_ffn_router_b"], np.float32)
    qw = np.asarray(inputs["L0_ffn_q_w"], np.float32)
    cols32 = np.arange(1024).reshape(16, 64)[:, :32].reshape(-1)
    ex = {}
    ex["fq_w32"] = np.ascontiguousarray(qw[:, cols32])
    qb32 = qb[cols32]
    qbT6 = np.zeros((128, 6), np.float32)
    kbT6 = np.zeros((128, 6), np.float32)
    for c in range(6):
        w = min(96, 512 - c * 96)
        qbT6[:w, c] = qb32[c * 96:c * 96 + w]
        kbT6[:w, c] = kb[c * 96:c * 96 + w]
    ex["fq_b32T"] = qbT6
    ex["fk_bT"] = kbT6
    ex["fo_bT"] = np.ascontiguousarray(ob.reshape(8, 128).T)
    b1T = np.zeros((128, 128), np.float32)
    for e in range(8):
        b1T[:, e * 16:(e + 1) * 16] = b1[e].reshape(16, 128).T
    ex["fb1T"] = b1T
    ex["fr_bB"] = np.ascontiguousarray(np.tile(rb.reshape(1, 8), (128, 1)))
    ex["oneb"] = np.ones((1, 128), ml_dtypes.bfloat16)
    return ex


def make_in_maps(inputs, S=None):
    B = inputs["L0_attn_x"].shape[0]
    Sfull = inputs["L0_attn_x"].shape[1] if S is None else S
    consts = host_consts(Sfull)
    extras = layout_extras(inputs)
    in_maps = []
    for b in range(B):
        m = {"x": np.ascontiguousarray(inputs["L0_attn_x"][b][:Sfull])}
        for wn, dn in zip(WNAMES, DNAMES):
            a = np.ascontiguousarray(np.asarray(inputs[wn]))
            if a.ndim == 1:
                a = a.reshape(1, -1)
            m[dn] = a
        m.update(consts)
        m.update(extras)
        in_maps.append(m)
    return in_maps


def kernel(**inputs):
    S = inputs["L0_attn_x"].shape[1]
    B = inputs["L0_attn_x"].shape[0]
    nc = build_program(S)
    in_maps = make_in_maps(inputs)
    res = run_bass_kernel_spmd(nc, in_maps, core_ids=list(range(B)))
    return np.stack([r["out"] for r in res.results], axis=0)
